# version 02
# baseline (speedup 1.0000x reference)
import contextlib
import numpy as np
import ml_dtypes
import concourse.bass as bass
import concourse.mybir as mybir
from concourse.bass_utils import run_bass_kernel_spmd

F32 = mybir.dt.float32
BF16 = mybir.dt.bfloat16
I32 = mybir.dt.int32
U8 = mybir.dt.uint8
AF = mybir.ActivationFunctionType
ALU = mybir.AluOpType
BF = ml_dtypes.bfloat16
AX = mybir.AxisListType.X

B, T, D = 4, 2048, 2048
DKh, DVh = 128, 128
HPC = 8
C = 64
NCH = T // C
LEVELS = [64, 32, 16, 8, 4, 2]
DSCALE = float(DKh) ** -0.5
TPC = 1024
EPS = 1e-5
LN2 = float(np.log(2.0))
LN1E4 = float(np.log(10000.0))
PI = float(np.pi)


class Dep:
    def __init__(self, serialize=False):
        self.sem = {}
        self.cnt = {}
        self.last_w = {}
        self.readers = {}
        self._issued = {}
        self.serialize = serialize

    def reg(self, name, sem):
        self.sem[name] = sem
        self.cnt[name] = 0

    def op(self, engname, eng, emit, reads=(), writes=(), force_same=False, inc=1):
        need = {}
        for b_ in reads:
            if b_ in self.last_w:
                en, c = self.last_w[b_]
                need[en] = max(need.get(en, 0), c)
        for b_ in writes:
            if b_ in self.last_w:
                en, c = self.last_w[b_]
                need[en] = max(need.get(en, 0), c)
            for (en, c) in self.readers.get(b_, []):
                need[en] = max(need.get(en, 0), c)
        if self.serialize:
            for en in self.sem:
                if self.cnt[en] > 0:
                    need[en] = self.cnt[en]
        for en, c in need.items():
            if c > 0:
                eng.wait_ge(self.sem[en], c)
        ins = emit()
        self.cnt[engname] += inc
        c = self.cnt[engname]
        ins.then_inc(self.sem[engname], inc)
        for b_ in writes:
            self.last_w[b_] = (engname, c)
            self.readers[b_] = []
        for b_ in reads:
            self.readers.setdefault(b_, []).append((engname, c))
        return ins

    def dma(self, engname, eng, out, in_, reads=(), writes=(), **kw):
        return self.op(engname, eng,
                       lambda: eng.dma_start(out=out, in_=in_, **kw),
                       reads=reads, writes=writes, inc=16)

    def fence(self, eng, engname):
        eng.wait_ge(self.sem[engname], self.cnt[engname])

    def flush_into(self, eng, engname):
        need = {}
        for b_, (en, c) in self.last_w.items():
            if en != engname:
                need[en] = max(need.get(en, 0), c)
        for en, c in need.items():
            eng.wait_ge(self.sem[en], c)
        self.fence(eng, engname)


class Prog:
    def __init__(self, name, serialize=True):
        self.nc = bass.Bass("TRN2", target_bir_lowering=False)
        self.ctx = contextlib.ExitStack()
        self.sb = {}
        self.ps = {}
        self.dep = Dep(serialize=serialize)
        self.name = name

    def engines(self):
        nc = self.nc
        return {"v": nc.vector, "s": nc.scalar, "t": nc.tensor, "y": nc.sync, "g": nc.gpsimd}

    def start(self):
        for s in ["v", "s", "t", "y", "g"]:
            self.dep.reg(s, self.ctx.enter_context(self.nc.semaphore("sem_" + s)))

    def sbuf(self, name, shape, dtype):
        self.sb[name] = self.ctx.enter_context(self.nc.sbuf_tensor("sb_" + name, shape, dtype))[:]
        return self.sb[name]

    def psum(self, name, shape, dtype=F32):
        self.ps[name] = self.ctx.enter_context(self.nc.psum_tensor("ps_" + name, shape, dtype))[:]
        return self.ps[name]

    def constcol(self, val):
        key = "cc_%r" % (val,)
        if key not in self.sb:
            V = self.engines()["v"]
            self.sbuf(key, [128, 1], F32)
            self.dep.op("v", V, lambda: V.memset(self.sb[key], val), writes=[key])
        return self.sb[key], key

    def finish(self):
        eng = self.engines()
        self.dep.flush_into(eng["y"], "y")
        self.dep.flush_into(eng["g"], "g")
        self.ctx.close()
        return self.nc


def rsqrt_act(P, out_ap, in_ap, eps, inv_n, reads, writes, scratch):
    dep = P.dep
    SC = P.engines()["s"]
    bcol, bkey = P.constcol(eps)
    dep.op("s", SC, lambda: SC.activation(P.sb[scratch], in_ap, AF.Ln, bias=bcol, scale=inv_n),
           reads=list(reads) + [bkey], writes=[scratch], force_same=True)
    dep.op("s", SC, lambda: SC.activation(out_ap, P.sb[scratch], AF.Exp, scale=-0.5),
           reads=[scratch], writes=list(writes), force_same=True)


def build_ln(quant):
    P = Prog("ln")
    nc = P.nc
    dx = nc.dram_tensor("x", [TPC, D], F32, kind="ExternalInput").ap()
    dxn = nc.dram_tensor("xn", [TPC, D], F32, kind="ExternalOutput").ap()
    if quant:
        dgam = nc.dram_tensor("gam", [1, D], F32, kind="ExternalInput").ap()
        dqa = nc.dram_tensor("qa", [TPC, D], F32, kind="ExternalOutput").ap()
        dds = nc.dram_tensor("ds0", [128, TPC // 128], F32, kind="ExternalOutput").ap()
    P.start()
    eng = P.engines()
    V, SC, TE, SY, GP = (eng[k] for k in "vstyg")
    dep = P.dep
    xt = P.sbuf("xt", [128, D], F32)
    dev = P.sbuf("dev", [128, D], F32)
    sq = P.sbuf("sq", [128, D], F32)
    xnt = P.sbuf("xnt", [128, D], F32)
    col = P.sbuf("col", [128, 1], F32)
    col2 = P.sbuf("col2", [128, 1], F32)
    P.sbuf("rsq_scr", [128, 1], F32)
    if quant:
        gamb = P.sbuf("gamb", [128, D], F32)
        ones1 = P.sbuf("ones1", [1, 128], F32)
        gamr = P.sbuf("gamr", [1, D], F32)
        xq = P.sbuf("xq", [128, D], F32)
        qi = P.sbuf("qi", [128, D], I32)
        qf = P.sbuf("qf", [128, D], F32)
        mx = P.sbuf("mx", [128, 1], F32)
        dsall = P.sbuf("dsall", [128, TPC // 128], F32)
        P.psum("pb", [128, 512], F32)
        dep.dma("y", SY, gamr, dgam[:, :], writes=["gamr"])
        dep.op("v", V, lambda: V.memset(ones1, 1.0), writes=["ones1"])
        dep.fence(V, "v")
        for ch in range(4):
            dep.op("t", TE, lambda ch=ch: TE.matmul(P.ps["pb"][:, 0:512], ones1,
                                                    gamr[:, 512 * ch:512 * (ch + 1)],
                                                    start=True, stop=True),
                   reads=["gamr", "ones1"], writes=["pb"])
            dep.op("v", V, lambda ch=ch: V.tensor_copy(gamb[:, 512 * ch:512 * (ch + 1)],
                                                       P.ps["pb"][:, 0:512]),
                   reads=["pb"], writes=["gamb"])
    for ti in range(TPC // 128):
        rs = slice(128 * ti, 128 * (ti + 1))
        dep.dma("y", SY, xt, dx[rs, :], writes=["xt"])
        dep.op("v", V, lambda: V.reduce_sum(col, xt, axis=AX), reads=["xt"], writes=["col"])
        dep.op("v", V, lambda: V.tensor_scalar(col, col, 1.0 / D, None, ALU.mult),
               reads=["col"], writes=["col"], force_same=True)
        dep.op("v", V, lambda: V.tensor_scalar(dev, xt, col, None, ALU.subtract),
               reads=["xt", "col"], writes=["dev"], force_same=True)
        dep.op("s", SC, lambda: SC.activation(sq, dev, AF.Square), reads=["dev"], writes=["sq"])
        dep.op("v", V, lambda: V.reduce_sum(col2, sq, axis=AX), reads=["sq"], writes=["col2"])
        rsqrt_act(P, col2, col2, EPS, 1.0 / D, ["col2"], ["col2"], "rsq_scr")
        dep.op("v", V, lambda: V.tensor_scalar(xnt, dev, col2, None, ALU.mult),
               reads=["dev", "col2"], writes=["xnt"], force_same=True)
        dep.dma("y", SY, dxn[rs, :], xnt, reads=["xnt"])
        if quant:
            dep.op("s", SC, lambda: SC.activation(sq, xnt, AF.Square), reads=["xnt"], writes=["sq"])
            dep.op("v", V, lambda: V.reduce_sum(col, sq, axis=AX), reads=["sq"], writes=["col"])
            rsqrt_act(P, col, col, EPS, 1.0 / D, ["col"], ["col"], "rsq_scr")
            dep.op("v", V, lambda: V.tensor_scalar(xq, xnt, col, None, ALU.mult),
                   reads=["xnt", "col"], writes=["xq"], force_same=True)
            dep.op("v", V, lambda: V.tensor_tensor(xq, xq, gamb, ALU.mult),
                   reads=["xq", "gamb"], writes=["xq"])
            dep.op("v", V, lambda: V.reduce_max(mx, xq, axis=AX, apply_absolute_value=True),
                   reads=["xq"], writes=["mx"])
            dep.op("v", V, lambda: V.tensor_scalar(mx, mx, 1e-5, None, ALU.max),
                   reads=["mx"], writes=["mx"], force_same=True)
            dep.op("v", V, lambda ti=ti: V.tensor_scalar(dsall[:, ti:ti + 1], mx, 1.0 / 127.0,
                                                         None, ALU.mult),
                   reads=["mx"], writes=["dsall"], force_same=True)
            dep.op("v", V, lambda: V.reciprocal(col, mx), reads=["mx"], writes=["col"],
                   force_same=True)
            dep.op("v", V, lambda: V.tensor_scalar(col, col, 127.0, None, ALU.mult),
                   reads=["col"], writes=["col"], force_same=True)
            dep.op("v", V, lambda: V.tensor_scalar(qi, xq, col, None, ALU.mult),
                   reads=["xq", "col"], writes=["qi"], force_same=True)
            dep.op("v", V, lambda: V.tensor_copy(qf, qi), reads=["qi"], writes=["qf"])
            dep.dma("y", SY, dqa[rs, :], qf, reads=["qf"])
    if quant:
        dep.dma("y", SY, dds[:, :], dsall, reads=["dsall"])
    return P.finish()


def build_l0proj():
    P = Prog("l0proj")
    nc = P.nc
    dxnT = nc.dram_tensor("xnT", [D, T], F32, kind="ExternalInput").ap()
    dwq = nc.dram_tensor("wq", [D, 1024], F32, kind="ExternalInput").ap()
    dwk = nc.dram_tensor("wk", [D, 1024], F32, kind="ExternalInput").ap()
    dwv = nc.dram_tensor("wv", [D, 1024], F32, kind="ExternalInput").ap()
    dwg = nc.dram_tensor("wg", [D, 1024], F32, kind="ExternalInput").ap()
    dwb = nc.dram_tensor("wb", [D, 8], F32, kind="ExternalInput").ap()
    dwf1 = nc.dram_tensor("wf1", [D, 128], F32, kind="ExternalInput").ap()
    dwf2 = nc.dram_tensor("wf2", [128, 1024], F32, kind="ExternalInput").ap()
    dcq = nc.dram_tensor("cq", [128, 32], F32, kind="ExternalInput").ap()
    dck = nc.dram_tensor("ck", [128, 32], F32, kind="ExternalInput").ap()
    dcv = nc.dram_tensor("cv", [128, 32], F32, kind="ExternalInput").ap()
    ddtb = nc.dram_tensor("dtb", [128, 8], F32, kind="ExternalInput").ap()
    dalog = nc.dram_tensor("alog", [1, 8], F32, kind="ExternalInput").ap()
    dbgv = nc.dram_tensor("bgv", [128, 8], F32, kind="ExternalInput").ap()
    dq8 = nc.dram_tensor("q8", [1024, T], F32, kind="ExternalOutput").ap()
    dk8 = nc.dram_tensor("k8", [1024, T], F32, kind="ExternalOutput").ap()
    dv8 = nc.dram_tensor("v8", [1024, T], F32, kind="ExternalOutput").ap()
    dld8 = nc.dram_tensor("ld8", [1024, T], F32, kind="ExternalOutput").ap()
    dgate8 = nc.dram_tensor("gate8", [1024, T], F32, kind="ExternalOutput").ap()
    dbeta8 = nc.dram_tensor("beta8", [8, T], F32, kind="ExternalOutput").ap()
    drq8 = nc.dram_tensor("rq8", [1, 8 * T], F32, kind="ExternalOutput").ap()
    P.start()
    eng = P.engines()
    V, SC, TE, SY, GP = (eng[k] for k in "vstyg")
    dep = P.dep
    xnb = P.sbuf("xnb", [128, 16 * T], BF16)
    wbuf = P.sbuf("wbuf", [128, 16 * 1024], BF16)
    xg1 = P.sbuf("xg1", [128, T], BF16)
    slab = P.sbuf("slab", [128, T + 4], F32)
    s2 = P.sbuf("s2", [128, T], F32)
    s3 = P.sbuf("s3", [128, T], F32)
    sqs = P.sbuf("sqs", [128, T], F32)
    rowb = P.sbuf("rowb", [1, T], F32)
    ones128 = P.sbuf("ones128", [128, 1], F32)
    ones1 = P.sbuf("ones1", [1, 128], F32)
    cw4 = P.sbuf("cw4", [128, 32], F32)
    dtbs = P.sbuf("dtbs", [128, 8], F32)
    bgs = P.sbuf("bgs", [128, 8], F32)
    negea = P.sbuf("negea", [1, 8], F32)
    neab = P.sbuf("neab", [128, 8], F32)
    P.sbuf("rsq_scr", [1, T], F32)
    pa = P.psum("pa", [128, 512], F32)
    pb = P.psum("pb", [128, 512], F32)
    pr = P.psum("pr", [1, 512], F32)
    pq = P.psum("pq", [8, 512], F32)

    dep.dma("g", GP, xnb.rearrange("p (dt t) -> p dt t", dt=16),
            dxnT.rearrange("(dt p) t -> p dt t", p=128), writes=["xnb"])
    dep.dma("y", SY, dtbs, ddtb[:, :], writes=["dtbs"])
    dep.dma("y", SY, bgs, dbgv[:, :], writes=["bgs"])
    dep.dma("y", SY, negea, dalog[:, :], writes=["negea"])
    dep.op("v", V, lambda: V.memset(ones1, 1.0), writes=["ones1"])
    dep.op("v", V, lambda: V.memset(ones128, 1.0), writes=["ones128"])
    dep.op("v", V, lambda: V.memset(slab[:, 0:4], 0.0), writes=["slabpad"])
    dep.fence(V, "v")
    dep.op("s", SC, lambda: SC.activation(negea, negea, AF.Exp),
           reads=["negea"], writes=["negea"], force_same=True)
    dep.op("v", V, lambda: V.tensor_scalar(negea, negea, -1.0, None, ALU.mult),
           reads=["negea"], writes=["negea"], force_same=True)
    dep.op("t", TE, lambda: TE.matmul(pb[:, 0:8], ones1, negea, start=True, stop=True),
           reads=["negea", "ones1"], writes=["pb"])
    dep.op("v", V, lambda: V.tensor_copy(neab, pb[:, 0:8]), reads=["pb"], writes=["neab"])

    def load_w(dw, key, ocols=1024):
        dep.dma("g", GP, wbuf[:, 0:16 * ocols].rearrange("p (dt o) -> p dt o", dt=16),
                dw.rearrange("(dt p) o -> p dt o", p=128), writes=["wbuf"])

    def head_mms(hs, tc, ocols=1024, nd=16, rhsbuf=None, rhskey="xnb", bank="pa"):
        rhsbuf = rhsbuf if rhsbuf is not None else xnb

        def mms():
            last = None
            for dt in range(nd):
                last = TE.matmul(P.ps[bank][:, 0:512],
                                 wbuf[:, dt * ocols + hs * 128: dt * ocols + hs * 128 + 128],
                                 rhsbuf[:, dt * T + tc * 512: dt * T + tc * 512 + 512],
                                 start=(dt == 0), stop=(dt == nd - 1))
            return last
        dep.op("t", TE, mms, reads=["wbuf", rhskey], writes=[bank])

    def conv_into_slab(hs):
        wc = lambda i: cw4[:, hs * 4 + i: hs * 4 + i + 1]
        dep.op("v", V, lambda: V.tensor_scalar(s2, slab[:, 1:T + 1], wc(0), None, ALU.mult),
               reads=["slab", "cw4"], writes=["s2"])
        dep.op("v", V, lambda: V.scalar_tensor_tensor(s3, slab[:, 2:T + 2], wc(1), s2,
                                                      ALU.mult, ALU.add),
               reads=["slab", "cw4", "s2"], writes=["s3"])
        dep.op("v", V, lambda: V.scalar_tensor_tensor(s2, slab[:, 3:T + 3], wc(2), s3,
                                                      ALU.mult, ALU.add),
               reads=["slab", "cw4", "s3"], writes=["s2"])
        dep.op("v", V, lambda: V.scalar_tensor_tensor(s3, slab[:, 4:T + 4], wc(3), s2,
                                                      ALU.mult, ALU.add),
               reads=["slab", "cw4", "s2"], writes=["s3"])
        dep.op("s", SC, lambda: SC.activation(s2, s3, AF.Silu), reads=["s3"], writes=["s2"])

    def sumsq_row():
        dep.op("s", SC, lambda: SC.activation(sqs, s2, AF.Square), reads=["s2"], writes=["sqs"])
        for ch in range(4):
            dep.op("t", TE, lambda ch=ch: TE.matmul(pr[:, 0:512], ones128,
                                                    sqs[:, 512 * ch:512 * (ch + 1)],
                                                    start=True, stop=True),
                   reads=["sqs", "ones128"], writes=["pr"])
            dep.op("v", V, lambda ch=ch: V.tensor_copy(rowb[:, 512 * ch:512 * (ch + 1)],
                                                       pr[:, 0:512]),
                   reads=["pr"], writes=["rowb"])
        b6, b6k = P.constcol(1e-6)
        dep.op("s", SC, lambda b6=b6: SC.activation(P.sb["rsq_scr"], rowb, AF.Ln,
                                                    bias=b6[0:1, :]),
               reads=["rowb", b6k], writes=["rsq_scr"])
        dep.op("s", SC, lambda: SC.activation(rowb, P.sb["rsq_scr"], AF.Exp, scale=-0.5),
               reads=["rsq_scr"], writes=["rowb"], force_same=True)

    for proj, dwx, dcx, outd, mode in [("q", dwq, dcq, dq8, "q"), ("k", dwk, dck, dk8, "k"),
                                       ("v", dwv, dcv, dv8, "v")]:
        load_w(dwx, "wbuf")
        dep.dma("y", SY, cw4, dcx[:, :], writes=["cw4"])
        for hs in range(HPC):
            for tc in range(4):
                head_mms(hs, tc)
                dep.op("v", V, lambda tc=tc: V.tensor_copy(slab[:, 4 + 512 * tc:4 + 512 * (tc + 1)],
                                                           pa[:, 0:512]),
                       reads=["pa"], writes=["slab"])
            conv_into_slab(hs)
            if mode == "k":
                sumsq_row()
                for ch in range(4):
                    dep.op("t", TE, lambda ch=ch: TE.matmul(pb[:, 0:512], ones1,
                                                            rowb[:, 512 * ch:512 * (ch + 1)],
                                                            start=True, stop=True),
                           reads=["rowb", "ones1"], writes=["pb"])
                    dep.op("v", V, lambda ch=ch: V.tensor_tensor(s3[:, 512 * ch:512 * (ch + 1)],
                                                                 s2[:, 512 * ch:512 * (ch + 1)],
                                                                 pb[:, 0:512], ALU.mult),
                           reads=["pb", "s2"], writes=["s3"])
                dep.dma("y", SY, outd[128 * hs:128 * (hs + 1), :], s3, reads=["s3"])
            elif mode == "q":
                sumsq_row()
                dep.dma("y", SY, drq8[:, T * hs:T * (hs + 1)], rowb, reads=["rowb"])
                dep.dma("y", SY, outd[128 * hs:128 * (hs + 1), :], s2, reads=["s2"])
            else:
                dep.dma("y", SY, outd[128 * hs:128 * (hs + 1), :], s2, reads=["s2"])
    load_w(dwg, "wbuf")
    for hs in range(HPC):
        for tc in range(4):
            head_mms(hs, tc)
            dep.op("v", V, lambda hs=hs, tc=tc: V.tensor_scalar(
                s2[:, 512 * tc:512 * (tc + 1)], pa[:, 0:512],
                bgs[:, hs:hs + 1], None, ALU.add),
                reads=["pa", "bgs"], writes=["s2"])
        dep.op("s", SC, lambda: SC.activation(s3, s2, AF.Silu), reads=["s2"], writes=["s3"])
        dep.dma("y", SY, dgate8[128 * hs:128 * (hs + 1), :], s3, reads=["s3"])
    load_w(dwf1, "wbuf", ocols=128)
    for tc in range(4):
        head_mms(0, tc, ocols=128)
        dep.op("v", V, lambda tc=tc: V.tensor_copy(xg1[:, 512 * tc:512 * (tc + 1)], pa[:, 0:512]),
               reads=["pa"], writes=["xg1"])
    dep.dma("g", GP, wbuf[:, 0:1024], dwf2[:, :], writes=["wbuf"])
    for hs in range(HPC):
        for tc in range(4):
            def mmg(hs=hs, tc=tc):
                return TE.matmul(pa[:, 0:512], wbuf[:, hs * 128:hs * 128 + 128],
                                 xg1[:, tc * 512:(tc + 1) * 512], start=True, stop=True)
            dep.op("t", TE, mmg, reads=["wbuf", "xg1"], writes=["pa"])
            sl = slice(512 * tc, 512 * (tc + 1))
            dep.op("v", V, lambda hs=hs, sl=sl: V.tensor_scalar(
                s2[:, sl], pa[:, 0:512], dtbs[:, hs:hs + 1], None, ALU.add),
                reads=["pa", "dtbs"], writes=["s2"])
        dep.op("s", SC, lambda: SC.activation(s3, s2, AF.Abs), reads=["s2"], writes=["s3"])
        dep.op("s", SC, lambda: SC.activation(s3, s3, AF.Exp, scale=-1.0),
               reads=["s3"], writes=["s3"], force_same=True)
        b1, b1k = P.constcol(1.0)
        dep.op("s", SC, lambda b1=b1: SC.activation(s3, s3, AF.Ln, bias=b1),
               reads=["s3", b1k], writes=["s3"], force_same=True)
        dep.op("v", V, lambda: V.tensor_scalar(sqs, s2, 0.0, None, ALU.max),
               reads=["s2"], writes=["sqs"])
        dep.op("v", V, lambda: V.tensor_tensor(s3, s3, sqs, ALU.add),
               reads=["s3", "sqs"], writes=["s3"])
        dep.op("v", V, lambda hs=hs: V.tensor_scalar(s3, s3, neab[:, hs:hs + 1], None, ALU.mult),
               reads=["s3", "neab"], writes=["s3"], force_same=True)
        dep.dma("y", SY, dld8[128 * hs:128 * (hs + 1), :], s3, reads=["s3"])
    dep.dma("g", GP, wbuf[:, 0:16 * 8].rearrange("p (dt o) -> p dt o", dt=16),
            dwb.rearrange("(dt p) o -> p dt o", p=128), writes=["wbuf"])
    for tc in range(4):
        def mmb(tc=tc):
            last = None
            for dt in range(16):
                last = TE.matmul(pq[:, 0:512], wbuf[:, dt * 8: dt * 8 + 8],
                                 xnb[:, dt * T + tc * 512: dt * T + tc * 512 + 512],
                                 start=(dt == 0), stop=(dt == 15))
            return last
        dep.op("t", TE, mmb, reads=["wbuf", "xnb"], writes=["pq"])
        dep.op("s", SC, lambda tc=tc: SC.activation(s2[0:8, 512 * tc:512 * (tc + 1)],
                                                    pq[:, 0:512], AF.Sigmoid),
               reads=["pq"], writes=["s2"])
    dep.dma("y", SY, dbeta8[:, :], s2[0:8, :], reads=["s2"])
    return P.finish()


def lvl_mask(L):
    t = np.arange(C)[:, None]
    s = np.arange(C)[None, :]
    m = ((t // L) == (s // L)) & ((t % L) >= L // 2) & ((s % L) < L // 2) & (s < t)
    return m.T


def build_delta(stage=9, serialize=True):
    P = Prog("delta", serialize=serialize)
    nc = P.nc
    dq = nc.dram_tensor("q8", [1024, T], F32, kind="ExternalInput").ap()
    dk = nc.dram_tensor("k8", [1024, T], F32, kind="ExternalInput").ap()
    dv = nc.dram_tensor("v8", [1024, T], F32, kind="ExternalInput").ap()
    dld = nc.dram_tensor("ld8", [1024, T], F32, kind="ExternalInput").ap()
    dbeta = nc.dram_tensor("beta64", [8 * C, NCH], F32, kind="ExternalInput").ap()
    drq = nc.dram_tensor("rqtok", [8 * 128, 16], F32, kind="ExternalInput").ap()
    dgate = nc.dram_tensor("gatetok", [8 * 128, T], F32, kind="ExternalInput").ap()
    donw = nc.dram_tensor("onw", [1, 128], F32, kind="ExternalInput").ap()
    don = nc.dram_tensor("on8", [8 * 128, T], F32, kind="ExternalOutput").ap()
    ident = nc.inline_tensor(np.eye(128, dtype=np.float32).astype(BF), "identc")
    identC = nc.inline_tensor(np.eye(C, dtype=np.float32).astype(BF), "identCc")
    maskts = {L: nc.inline_tensor(np.tile(lvl_mask(L), (1, 8)).astype(np.uint8), f"m{L}c")
              for L in LEVELS}
    maskD = nc.inline_tensor(np.tile(np.eye(C, dtype=bool), (1, 8)).astype(np.uint8), "mDc")
    _t = np.arange(C)[:, None]
    _s = np.arange(C)[None, :]
    lowT = nc.inline_tensor(np.tile((_s < _t).T.astype(np.uint8), (1, 8)), "lowTc")
    lowG = nc.inline_tensor(np.tile((_s <= _t).T.astype(np.uint8), (1, 8)), "lowGc")
    P.start()
    eng = P.engines()
    V, SC, TE, SY, GP = (eng[k] for k in "vstyg")
    dep = P.dep
    for nm, shp, dt in ([
        ("qT", [DKh, T], BF16), ("kT", [DKh, T], BF16), ("vT", [DKh, T], BF16),
        ("ldT", [DKh, T], F32), ("cumT", [DKh, T], F32),
        ("Erow", [DKh, T], BF16),
        ("rowK", [DKh, T], BF16), ("rowQ", [DKh, T], BF16), ("colK", [DKh, T], BF16),
        ("Kt", [DKh, T], BF16), ("Qt", [DKh, T], BF16), ("Kbar", [DKh, T], BF16),
        ("Dall", [DKh, NCH], F32), ("beta64", [C, NCH], F32), ("betaneg", [C, NCH], F32),
        ("TstA", [C, T], BF16), ("TstB", [C, T], BF16),
        ("GstA", [C, T], BF16), ("GstB", [C, T], BF16),
        ("Nst", [C, T], BF16), ("Nnat", [C, T], BF16),
        ("N2st", [C, T], BF16), ("N2nat", [C, T], BF16),
        ("N3st", [C, T], BF16), ("N3nat", [C, T], BF16),
        ("YA", [C, NCH * 256], BF16), ("YB", [C, NCH * 256], BF16),
        ("U0X", [C, NCH * 256], BF16),
        ("Kbtok", [C, NCH * DKh], BF16),
        ("Qjs", [DKh, NCH * DVh], BF16), ("nYT", [DKh, NCH * DKh], BF16),
        ("O0s", [C, NCH * DVh], F32), ("Fs", [DKh, NCH * C], BF16),
        ("S_A", [DKh, DVh], F32), ("S_B", [DKh, DVh], F32), ("Sbf", [DKh, DVh], BF16),
        ("o_slab", [128, T], F32),
        ("rqt", [128, 16], F32),
        ("onwb", [128, 128], F32),
        ("onwr", [1, 128], F32),
        ("ocol", [128, 1], F32), ("osq32", [128, 128], F32),
        ("rsq_scr", [128, 1], F32),
        ("ident", [128, 128], BF16), ("identC", [C, C], BF16),
        ("ones64", [DKh, C], F32), ("ones1", [1, 128], F32),
    ] + [(f"mask{L}", [C, 8 * C], U8) for L in LEVELS]
        + [("maskD", [C, 8 * C], U8), ("lowT", [C, 8 * C], U8), ("lowG", [C, 8 * C], U8),
           ("zeros8", [C, 8 * C], BF16)]):
        P.sbuf(nm, shp, dt)
    sb = P.sb
    for nm, shp, dt in [("bankA", [C, 512], F32), ("bankB", [C, 512], F32),
                        ("bankT", [128, 1024], BF16), ("bankT2", [128, 1024], BF16),
                        ("bankS", [C, 1024], F32), ("bankS2", [DKh, 512], F32),
                        ("bankP", [DKh, 512], F32)]:
        P.psum(nm, shp, dt)
    ps = P.ps
    dep.dma("y", SY, sb["ident"], ident.ap()[:, :], writes=["ident"])
    dep.dma("y", SY, sb["identC"], identC.ap()[:, :], writes=["identC"])
    for L in LEVELS:
        dep.dma("y", SY, sb[f"mask{L}"], maskts[L].ap()[:, :], writes=[f"mask{L}"])
    dep.dma("y", SY, sb["maskD"], maskD.ap()[:, :], writes=["maskD"])
    dep.dma("y", SY, sb["lowT"], lowT.ap()[:, :], writes=["lowT"])
    dep.dma("y", SY, sb["lowG"], lowG.ap()[:, :], writes=["lowG"])
    dep.dma("y", SY, sb["onwr"], donw[:, :], writes=["onwr"])
    dep.op("v", V, lambda: V.memset(sb["zeros8"], 0.0), writes=["zeros8"])
    dep.op("v", V, lambda: V.memset(sb["ones64"], 1.0), writes=["ones64"])
    dep.op("v", V, lambda: V.memset(sb["ones1"], 1.0), writes=["ones1"])
    dep.fence(V, "v")
    dep.op("t", TE, lambda: TE.matmul(ps["bankP"][:, 0:128], sb["ones1"], sb["onwr"],
                                      start=True, stop=True),
           reads=["onwr", "ones1"], writes=["bankP"])
    dep.op("v", V, lambda: V.tensor_copy(sb["onwb"], ps["bankP"][:, 0:128]),
           reads=["bankP"], writes=["onwb"])

    def A3(ap, L):
        return ap.rearrange("p (n l) -> p n l", l=L)

    for h in range(HPC):
        hsl = slice(128 * h, 128 * (h + 1))
        dep.dma("g", GP, sb["qT"], dq[hsl, :], writes=["qT"])
        dep.dma("g", GP, sb["kT"], dk[hsl, :], writes=["kT"])
        dep.dma("g", GP, sb["vT"], dv[hsl, :], writes=["vT"])
        dep.dma("y", SY, sb["ldT"], dld[hsl, :], writes=["ldT", "scrb", "gslab"])
        dep.dma("y", SY, sb["beta64"], dbeta[C * h:C * (h + 1), :], writes=["beta64"])
        dep.dma("y", SY, sb["rqt"], drq[128 * h:128 * (h + 1), :], writes=["rqt"])
        dep.op("v", V, lambda: V.memset(sb["S_A"], 0.0), writes=["S_A"])
        dep.op("v", V, lambda: V.tensor_scalar(sb["betaneg"], sb["beta64"], -1.0, None, ALU.mult),
               reads=["beta64"], writes=["betaneg"])
        dep.fence(V, "v")
        for j in range(NCH):
            dep.op("v", V, lambda j=j: V.tensor_tensor_scan(
                sb["cumT"][:, C * j:C * (j + 1)], sb["ones64"],
                sb["ldT"][:, C * j:C * (j + 1)], 0.0, ALU.mult, ALU.add),
                reads=["ldT", "ones64"], writes=["cum"])
        cum_ends = A3(sb["cumT"], C)[:, :, C - 1]
        dep.op("s", SC, lambda: SC.activation(sb["Dall"], cum_ends, AF.Exp),
               reads=["cum"], writes=["Dall"])
        dep.op("s", SC, lambda: SC.activation(sb["Erow"], sb["cumT"], AF.Exp),
               reads=["cum"], writes=["Erow"])
        dep.op("v", V, lambda: V.tensor_tensor(sb["Kt"], sb["kT"], sb["Erow"], ALU.mult),
               reads=["Erow", "kT"], writes=["Kt"])
        dep.op("v", V, lambda: V.tensor_tensor(sb["Qt"], sb["qT"], sb["Erow"], ALU.mult),
               reads=["Erow", "qT"], writes=["Qt"])
        cumC_b = A3(sb["cumT"], C)[:, :, C - 1:C].broadcast_to((DKh, NCH, C))
        dep.op("v", V, lambda: V.scalar_tensor_tensor(A3(sb["ldT"], C), A3(sb["cumT"], C),
                                                      -1.0, cumC_b, ALU.mult, ALU.add),
               reads=["cum"], writes=["ldT", "scrb"])
        dep.op("s", SC, lambda: SC.activation(sb["Erow"], sb["ldT"], AF.Exp),
               reads=["scrb"], writes=["Erow"])
        dep.op("v", V, lambda: V.tensor_tensor(sb["Kbar"], sb["kT"], sb["Erow"], ALU.mult),
               reads=["Erow", "kT"], writes=["Kbar"])
        if stage == 1:
            dep.dma("g", GP, don[128 * h:128 * (h + 1), :], sb["Kbar"], reads=["Kbar"])
            continue
        phase = {"T": 0, "G": 0}

        def sel_names(which):
            cur = phase[which]
            phase[which] = 1 - cur
            return (f"{which}st{'AB'[1 - cur]}", f"{which}st{'AB'[cur]}")

        for li, L in enumerate(LEVELS):
            Av = A3(sb["cumT"], L)[:, :, L // 2 - 1:L // 2].broadcast_to((DKh, T // L, L))
            dep.op("v", V, lambda Av=Av, L=L: V.scalar_tensor_tensor(
                A3(sb["ldT"], L), Av, -1.0, A3(sb["cumT"], L), ALU.mult, ALU.add),
                reads=["cum"], writes=["ldT", "scrb"])
            dep.op("s", SC, lambda: SC.activation(sb["Erow"], sb["ldT"], AF.Exp, scale=-1.0),
                   reads=["scrb"], writes=["Erow"])
            dep.op("v", V, lambda: V.tensor_tensor(sb["colK"], sb["kT"], sb["Erow"], ALU.mult),
                   reads=["Erow", "kT"], writes=["colK"])
            dep.op("s", SC, lambda: SC.activation(sb["Erow"], sb["ldT"], AF.Exp),
                   reads=["scrb"], writes=["Erow"])
            for nm, src in [("rowK", "kT"), ("rowQ", "qT")]:
                dep.op("v", V, lambda nm=nm, src=src: V.tensor_tensor(
                    sb[nm], sb[src], sb["Erow"], ALU.mult), reads=["Erow", src], writes=[nm])
            selT = sel_names("T")
            selG = sel_names("G")
            for jg in range(NCH // 8):
                for which, rowbuf, bank, (dst, src) in [("T", "rowK", "bankA", selT),
                                                        ("G", "rowQ", "bankB", selG)]:
                    def mms(jg=jg, rowbuf=rowbuf, bank=bank):
                        last = None
                        for c2 in range(8):
                            j = jg * 8 + c2
                            sl = slice(C * j, C * (j + 1))
                            last = TE.matmul(ps[bank][:, C * c2:C * (c2 + 1)],
                                             sb["colK"][:, sl], sb[rowbuf][:, sl],
                                             start=True, stop=True)
                        return last
                    dep.op("t", TE, mms, reads=["colK", rowbuf], writes=[bank])
                    gsl = slice(C * 8 * jg, C * 8 * (jg + 1))
                    if li == 0:
                        dep.op("v", V, lambda dst=dst, gsl=gsl, bank=bank, L=L: V.tensor_tensor(
                            sb[dst][:, gsl], ps[bank][:, :], sb[f"mask{L}"], ALU.mult),
                            reads=[bank, f"mask{L}"], writes=[dst + "g%d" % jg])
                    else:
                        dep.op("v", V, lambda dst=dst, src=src, gsl=gsl, bank=bank, L=L: V.select(
                            sb[dst][:, gsl], sb[f"mask{L}"], ps[bank][:, :], sb[src][:, gsl]),
                            reads=[bank, f"mask{L}", src + "g%d" % jg], writes=[dst + "g%d" % jg])
        dstD, srcD = sel_names("G")
        for jg in range(NCH // 8):
            def mms(jg=jg):
                last = None
                for c2 in range(8):
                    j = jg * 8 + c2
                    sl = slice(C * j, C * (j + 1))
                    last = TE.matmul(ps["bankB"][:, C * c2:C * (c2 + 1)],
                                     sb["kT"][:, sl], sb["qT"][:, sl], start=True, stop=True)
                return last
            dep.op("t", TE, mms, reads=["kT", "qT"], writes=["bankB"])
            gsl = slice(C * 8 * jg, C * 8 * (jg + 1))
            dep.op("v", V, lambda gsl=gsl: V.select(
                sb[dstD][:, gsl], sb["maskD"], ps["bankB"][:, :], sb[srcD][:, gsl]),
                reads=["bankB", "maskD", srcD + "g%d" % jg], writes=[dstD + "g%d" % jg])
        selTf = sel_names("T")
        selGf = sel_names("G")
        for which, mk, (dst, src) in [("T", "lowT", selTf), ("G", "lowG", selGf)]:
            for jg in range(NCH // 8):
                gsl = slice(C * 8 * jg, C * 8 * (jg + 1))
                dep.op("v", V, lambda dst=dst, src=src, gsl=gsl, mk=mk: V.select(
                    sb[dst][:, gsl], sb[mk], sb[src][:, gsl], sb["zeros8"]),
                    reads=[mk, src + "g%d" % jg, "zeros8"], writes=[dst + "g%d" % jg])
        Tfin = f"Tst{'AB'[phase['T']]}"
        Gfin = f"Gst{'AB'[phase['G']]}"
        tread = [Tfin + "g%d" % g for g in range(4)]
        gread = [Gfin + "g%d" % g for g in range(4)]
        if stage == 2:
            dep.dma("g", GP, don[128 * h:128 * h + 64, :], sb[Tfin], reads=tread)
            dep.dma("g", GP, don[128 * h + 64:128 * (h + 1), :], sb[Gfin], reads=gread)
            continue
        bneg = sb["betaneg"].rearrange("p (c u) -> p c u", u=1).broadcast_to((C, NCH, C))
        dep.op("v", V, lambda: V.tensor_tensor(
            sb["Nst"].rearrange("p (c t) -> p c t", t=C),
            sb[Tfin].rearrange("p (c t) -> p c t", t=C), bneg, ALU.mult),
            reads=tread + ["betaneg"], writes=["Nst"])
        for j in range(NCH):
            sl = slice(C * j, C * (j + 1))
            dep.op("t", TE, lambda sl=sl: TE.transpose(ps["bankT"][0:C, 0:DVh], sb["vT"][:, sl],
                                                       sb["ident"]),
                   reads=["vT", "ident"], writes=["bankT"])
            dep.op("v", V, lambda j=j: V.tensor_copy(sb["YA"][:, 256 * j:256 * j + DVh],
                                                     ps["bankT"][0:C, 0:DVh]),
                   reads=["bankT"], writes=["YA"])
            dep.op("t", TE, lambda sl=sl: TE.transpose(ps["bankT2"][0:C, 0:DKh], sb["Kt"][:, sl],
                                                       sb["ident"]),
                   reads=["Kt", "ident"], writes=["bankT2"])
            dep.op("v", V, lambda j=j: V.tensor_copy(sb["YA"][:, 256 * j + DVh:256 * (j + 1)],
                                                     ps["bankT2"][0:C, 0:DKh]),
                   reads=["bankT2"], writes=["YA"])
            dep.op("t", TE, lambda sl=sl: TE.transpose(ps["bankT"][0:C, 0:DKh], sb["Kbar"][:, sl],
                                                       sb["ident"]),
                   reads=["Kbar", "ident"], writes=["bankT"])
            dep.op("v", V, lambda j=j: V.tensor_copy(sb["Kbtok"][:, DKh * j:DKh * (j + 1)],
                                                     ps["bankT"][0:C, 0:DKh]),
                   reads=["bankT"], writes=["Kbtok"])
            dep.op("t", TE, lambda sl=sl: TE.transpose(ps["bankT2"][0:C, 0:C], sb["Nst"][:, sl],
                                                       sb["identC"]),
                   reads=["Nst", "identC"], writes=["bankT2"])
            dep.op("v", V, lambda sl=sl: V.tensor_copy(sb["Nnat"][:, sl], ps["bankT2"][0:C, 0:C]),
                   reads=["bankT2"], writes=["Nnat"])
        if stage == 25:
            dep.dma("g", GP, don[128 * h:128 * h + 64, 0:T], sb["YA"][:, 0:T], reads=["YA"])
            dep.dma("g", GP, don[128 * h + 64:128 * (h + 1), 0:T],
                    sb["YA"][:, 2 * T:3 * T], reads=["YA"])
            continue
        ycur, yoth = "YA", "YB"
        pows = [("Nst", "Nnat"), ("N2st", "N2nat"), ("N3st", "N3nat")]
        cur_pow = 0
        pst, pnat = pows[0]
        for it in range(6):
            for jg in range(NCH // 4):
                def mms(jg=jg, pst=pst, ycur=ycur):
                    last = None
                    for c2 in range(4):
                        j = jg * 4 + c2
                        last = TE.matmul(ps["bankS"][:, 256 * c2:256 * (c2 + 1)],
                                         sb[pst][:, C * j:C * (j + 1)],
                                         sb[ycur][:, 256 * j:256 * (j + 1)],
                                         start=True, stop=True)
                    return last
                dep.op("t", TE, mms, reads=[pst, ycur], writes=["bankS"])
                for half in range(2):
                    gs2 = slice(256 * 4 * jg + 512 * half, 256 * 4 * jg + 512 * (half + 1))
                    dep.op("v", V, lambda gs2=gs2, half=half, yoth=yoth, ycur=ycur: V.tensor_tensor(
                        sb[yoth][:, gs2], ps["bankS"][:, 512 * half:512 * (half + 1)],
                        sb[ycur][:, gs2], ALU.add),
                        reads=["bankS", ycur], writes=[yoth])
            ycur, yoth = yoth, ycur
            if stage == 26 and it == 0:
                break
            if it < 5:
                nst, nnat = pows[(cur_pow + 1) % 3]
                for jg in range(NCH // 8):
                    for (dstb, l_, r_) in [("bankA", pnat, pst), ("bankB", pst, pnat)]:
                        def mms(jg=jg, dstb=dstb, l_=l_, r_=r_):
                            last = None
                            for c2 in range(8):
                                j = jg * 8 + c2
                                sl = slice(C * j, C * (j + 1))
                                last = TE.matmul(ps[dstb][:, C * c2:C * (c2 + 1)],
                                                 sb[l_][:, sl], sb[r_][:, sl],
                                                 start=True, stop=True)
                            return last
                        dep.op("t", TE, mms, reads=[l_, r_], writes=[dstb])
                    gsl = slice(C * 8 * jg, C * 8 * (jg + 1))
                    dep.op("v", V, lambda gsl=gsl, nst=nst: V.tensor_copy(sb[nst][:, gsl],
                                                                          ps["bankA"][:, :]),
                           reads=["bankA"], writes=[nst])
                    dep.op("v", V, lambda gsl=gsl, nnat=nnat: V.tensor_copy(sb[nnat][:, gsl],
                                                                            ps["bankB"][:, :]),
                           reads=["bankB"], writes=[nnat])
                pst, pnat = nst, nnat
                cur_pow = (cur_pow + 1) % 3
            if stage == 27 and it == 0:
                break
        bpos = sb["beta64"].rearrange("p (c u) -> p c u", u=1).broadcast_to((C, NCH, 256))
        dep.op("v", V, lambda ycur=ycur: V.tensor_tensor(
            sb["U0X"].rearrange("p (c t) -> p c t", t=256),
            sb[ycur].rearrange("p (c t) -> p c t", t=256), bpos, ALU.mult),
            reads=[ycur, "beta64"], writes=["U0X"])
        if stage == 3:
            dep.dma("g", GP, don[128 * h:128 * h + 64, 0:T], sb["U0X"][:, 0:T], reads=["U0X"])
            dep.dma("g", GP, don[128 * h + 64:128 * (h + 1), 0:T],
                    sb["U0X"][:, T:2 * T], reads=["U0X"])
            continue
        for j in range(NCH):
            u0 = sb["U0X"][:, 256 * j:256 * j + DVh]
            xx = sb["U0X"][:, 256 * j + DVh:256 * (j + 1)]
            kb = sb["Kbtok"][:, DKh * j:DKh * (j + 1)]
            gsl2 = sb[Gfin][:, C * j:C * (j + 1)]

            def mm2(u0=u0, xx=xx, kb=kb):
                TE.matmul(ps["bankP"][:, 0:DVh], kb, u0, start=True, stop=True)
                return TE.matmul(ps["bankP"][:, DVh:DVh + DKh], xx, kb, start=True, stop=True)
            dep.op("t", TE, mm2, reads=["Kbtok", "U0X"], writes=["bankP"])
            dep.op("v", V, lambda j=j: V.tensor_copy(sb["Qjs"][:, DVh * j:DVh * (j + 1)],
                                                     ps["bankP"][:, 0:DVh]),
                   reads=["bankP"], writes=["Qjs"])
            dep.op("v", V, lambda j=j: V.tensor_scalar(sb["nYT"][:, DKh * j:DKh * (j + 1)],
                                                       ps["bankP"][:, DVh:DVh + DKh],
                                                       -1.0, None, ALU.mult),
                   reads=["bankP"], writes=["nYT"])

            def mm3(u0=u0, xx=xx, gsl2=gsl2):
                TE.matmul(ps["bankS"][0:C, 0:DVh], gsl2, u0, start=True, stop=True)
                return TE.matmul(ps["bankS2"][0:DKh, 0:C], xx, gsl2, start=True, stop=True)
            dep.op("t", TE, mm3, reads=gread + ["U0X"], writes=["bankS", "bankS2"])
            dep.op("v", V, lambda j=j: V.tensor_scalar(sb["O0s"][:, DVh * j:DVh * (j + 1)],
                                                       ps["bankS"][0:C, 0:DVh],
                                                       DSCALE, None, ALU.mult),
                   reads=["bankS"], writes=["O0s"])
            dep.op("v", V, lambda j=j: V.tensor_tensor(sb["Fs"][:, C * j:C * (j + 1)],
                                                       sb["Qt"][:, C * j:C * (j + 1)],
                                                       ps["bankS2"][0:DKh, 0:C], ALU.subtract),
                   reads=["bankS2", "Qt"], writes=["Fs"])
        if stage in (26, 27):
            dep.dma("g", GP, don[128 * h:128 * h + 64, 0:T], sb[ycur][:, 0:T], reads=[ycur])
            dep.dma("g", GP, don[128 * h + 64:128 * (h + 1), 0:T],
                    sb["N2st" if stage == 27 else "Nst"][:, 0:T],
                    reads=["N2st" if stage == 27 else "Nst"])
            continue
        if stage == 4:
            dep.dma("g", GP, don[128 * h:128 * (h + 1), 0:T], sb["Qjs"][:, 0:T], reads=["Qjs"])
            continue
        scur, soth = "S_A", "S_B"
        for j in range(NCH):
            dep.op("v", V, lambda scur=scur: V.tensor_copy(sb["Sbf"], sb[scur]),
                   reads=[scur], writes=["Sbf"])

            def mmS(j=j):
                TE.matmul(ps["bankP"][:, 0:DVh], sb["nYT"][:, DKh * j:DKh * (j + 1)], sb["Sbf"],
                          start=True, stop=False)
                return TE.matmul(ps["bankP"][:, 0:DVh], sb["ident"],
                                 sb["Qjs"][:, DVh * j:DVh * (j + 1)], start=False, stop=True)
            dep.op("t", TE, mmS, reads=["nYT", "Sbf", "Qjs", "ident"], writes=["bankP"])
            dep.op("t", TE, lambda j=j: TE.matmul(ps["bankS"][0:C, 0:DVh],
                                                  sb["Fs"][:, C * j:C * (j + 1)], sb["Sbf"],
                                                  start=True, stop=True),
                   reads=["Fs", "Sbf"], writes=["bankS"])
            dcol = sb["Dall"][:, j:j + 1]
            dep.op("v", V, lambda scur=scur, soth=soth, dcol=dcol: V.scalar_tensor_tensor(
                sb[soth], sb[scur], dcol, ps["bankP"][:, 0:DVh], ALU.mult, ALU.add),
                reads=["bankP", scur, "Dall"], writes=[soth])
            tile = j // 2
            pbs = (j % 2) * C
            dep.op("v", V, lambda j=j, tile=tile, pbs=pbs: V.scalar_tensor_tensor(
                sb["o_slab"][pbs:pbs + C, DVh * tile:DVh * (tile + 1)],
                ps["bankS"][0:C, 0:DVh], DSCALE,
                sb["O0s"][:, DVh * j:DVh * (j + 1)], ALU.mult, ALU.add),
                reads=["bankS", "O0s"], writes=["o_slab"])
            scur, soth = soth, scur
        if stage == 5:
            dep.dma("g", GP, don[128 * h:128 * (h + 1), :], sb["o_slab"], reads=["o_slab"])
            continue
        dep.dma("y", SY, sb["ldT"], dgate[128 * h:128 * (h + 1), :],
                writes=["ldT", "scrb", "gslab"])
        for ti in range(16):
            osl = sb["o_slab"][:, 128 * ti:128 * (ti + 1)]
            gsl3 = sb["ldT"][:, 128 * ti:128 * (ti + 1)]
            rqc = sb["rqt"][:, ti:ti + 1]
            dep.op("v", V, lambda osl=osl, rqc=rqc: V.tensor_scalar(osl, osl, rqc, None, ALU.mult),
                   reads=["o_slab", "rqt"], writes=["o_slab"], force_same=True)
            dep.op("s", SC, lambda osl=osl: SC.activation(sb["osq32"], osl, AF.Square),
                   reads=["o_slab"], writes=["osq32"])
            dep.op("v", V, lambda: V.reduce_sum(sb["ocol"], sb["osq32"], axis=AX),
                   reads=["osq32"], writes=["ocol"])
            rsqrt_act(P, sb["ocol"], sb["ocol"], EPS, 1.0 / DVh, ["ocol"], ["ocol"], "rsq_scr")
            dep.op("v", V, lambda osl=osl: V.tensor_scalar(osl, osl, sb["ocol"], None, ALU.mult),
                   reads=["o_slab", "ocol"], writes=["o_slab"], force_same=True)
            dep.op("v", V, lambda osl=osl: V.tensor_tensor(osl, osl, sb["onwb"], ALU.mult),
                   reads=["o_slab", "onwb"], writes=["o_slab"])
            dep.op("v", V, lambda osl=osl, gsl3=gsl3: V.tensor_tensor(osl, osl, gsl3, ALU.mult),
                   reads=["o_slab", "gslab"], writes=["o_slab"])
        dep.dma("y", SY, don[128 * h:128 * (h + 1), :], sb["o_slab"], reads=["o_slab"])
    return P.finish()


def build_ffn0(dbg=0):
    P = Prog("ffn0")
    nc = P.nc
    dxnT = nc.dram_tensor("xnT", [D, TPC], F32, kind="ExternalInput").ap()
    dxT = nc.dram_tensor("xT", [D, TPC], F32, kind="ExternalInput").ap()
    donT = nc.dram_tensor("onT", [D, TPC], F32, kind="ExternalInput").ap()
    dwo0 = nc.dram_tensor("wo0", [D, D], F32, kind="ExternalInput").ap()
    dgwT = nc.dram_tensor("gwT", [D, 32], F32, kind="ExternalInput").ap()
    dgb = nc.dram_tensor("gb", [1, 32], F32, kind="ExternalInput").ap()
    dfc1T = nc.dram_tensor("fc1T", [D, 1024], F32, kind="ExternalInput").ap()
    dw1T = nc.dram_tensor("w1T", [32, 1024, 1024], F32, kind="ExternalInput").ap()
    dw2T = nc.dram_tensor("w2T", [32, 1024, 1024], F32, kind="ExternalInput").ap()
    dfc2T = nc.dram_tensor("fc2T", [1024, D], F32, kind="ExternalInput").ap()
    dsuT = nc.dram_tensor("suT", [D, 4096], F32, kind="ExternalInput").ap()
    dsdT = nc.dram_tensor("sdT", [4096, D], F32, kind="ExternalInput").ap()
    dx1T = nc.dram_tensor("x1T", [D, TPC], F32, kind="ExternalOutput").ap()
    dcwT = nc.dram_tensor("cwT_d", [32, TPC], F32, kind="Internal").ap()
    if dbg >= 1:
        ddbg_cw = nc.dram_tensor("dbg_cw", [32, TPC], F32, kind="ExternalOutput").ap()
    if dbg >= 2:
        ddbg_xlat = nc.dram_tensor("dbg_xlat", [1024, TPC], F32, kind="ExternalOutput").ap()
    if dbg >= 3:
        ddbg_ylat = nc.dram_tensor("dbg_ylat", [1024, TPC], F32, kind="ExternalOutput").ap()
    if dbg >= 4:
        ddbg_g0s = nc.dram_tensor("dbg_g0s", [D, TPC], F32, kind="ExternalOutput").ap()
        ddbg_s0 = nc.dram_tensor("dbg_s0", [D, TPC], F32, kind="ExternalOutput").ap()
    P.start()
    eng = P.engines()
    V, SC, TE, SY, GP = (eng[k] for k in "vstyg")
    dep = P.dep
    xn = P.sbuf("xn", [128, 16 * TPC], BF16)
    xtile = P.sbuf("xtile", [128, 512], F32)
    onc = P.sbuf("onc", [128, 16 * 512], BF16)
    xlat = P.sbuf("xlat", [128, 8 * TPC], BF16)
    wb = P.sbuf("wb", [128, 16 * 1024], BF16)
    hbuf = P.sbuf("hbuf", [128, 8 * 512], BF16)
    rscr = P.sbuf("rscr", [128, 512], F32)
    rscr2 = P.sbuf("rscr2", [128, 512], F32)
    yaccA = P.sbuf("yaccA", [128, 8 * 512], F32)
    yaccB = P.sbuf("yaccB", [128, 8 * 512], F32)
    ylbf = P.sbuf("ylbf", [128, 8 * 512], BF16)
    shs = P.sbuf("shs", [128, 32 * 512], BF16)
    sco = P.sbuf("sco", [128, 32], F32)
    sbb = P.sbuf("sbb", [128, 32], F32)
    msk = P.sbuf("msk", [128, 32], F32)
    gbb = P.sbuf("gbb", [128, 32], F32)
    gbr = P.sbuf("gbr", [1, 32], F32)
    ones1 = P.sbuf("ones1", [1, 128], F32)
    mcol = P.sbuf("mcol", [128, 1], F32)
    wcol = P.sbuf("wcol", [128, 1], F32)
    cwt = P.sbuf("cwt", [32, TPC], F32)
    cwrow = P.sbuf("cwrow", [1, 512], F32)
    cwsel = P.sbuf("cwsel", [128, 32], F32)
    g0s = P.sbuf("g0s", [128, 512], F32)
    pa = P.psum("pa", [128, 512], F32)
    pb = P.psum("pb", [128, 512], F32)
    pt = P.psum("pt", [128, 1024], F32)
    dep.dma("g", GP, xn.rearrange("p (dt t) -> p dt t", dt=16),
            dxnT.rearrange("(dt p) t -> p dt t", p=128), writes=["xn"])
    dep.dma("y", SY, gbr, dgb[:, :], writes=["gbr"])
    dep.op("v", V, lambda: V.memset(ones1, 1.0), writes=["ones1"])
    dep.fence(V, "v")
    dep.op("t", TE, lambda: TE.matmul(pa[:, 0:32], ones1, gbr, start=True, stop=True),
           reads=["gbr", "ones1"], writes=["pa"])
    dep.op("v", V, lambda: V.tensor_copy(gbb, pa[:, 0:32]), reads=["pa"], writes=["gbb"])
    dep.dma("g", GP, wb[:, 0:16 * 32].rearrange("p (dt o) -> p dt o", dt=16),
            dgwT.rearrange("(dt p) o -> p dt o", p=128), writes=["wb"])
    for ti in range(8):
        def mml(ti=ti):
            last = None
            for dt in range(16):
                last = TE.matmul(pa[:, 0:32], xn[:, dt * TPC + ti * 128: dt * TPC + ti * 128 + 128],
                                 wb[:, dt * 32: dt * 32 + 32], start=(dt == 0), stop=(dt == 15))
            return last
        dep.op("t", TE, mml, reads=["wb", "xn"], writes=["pa"])
        dep.op("s", SC, lambda: SC.activation(sco, pa[:, 0:32], AF.Sigmoid),
               reads=["pa"], writes=["sco"])
        dep.op("v", V, lambda: V.tensor_tensor(sbb, sco, gbb, ALU.add),
               reads=["sco", "gbb"], writes=["sbb"])
        sb3 = sbb.rearrange("p (g e) -> p g e", g=4)
        m3 = msk.rearrange("p (g e) -> p g e", g=4)
        if "gmx" not in P.sb:
            P.sbuf("gmx", [128, 4], F32)
            P.sbuf("gmx2", [128, 4], F32)
            P.sbuf("gsum", [128, 4], F32)
            P.sbuf("gthr", [128, 1], F32)
        gmx, gmx2, gsum, gthr = P.sb["gmx"], P.sb["gmx2"], P.sb["gsum"], P.sb["gthr"]
        dep.op("v", V, lambda sb3=sb3: V.reduce_max(gmx, sb3, axis=AX),
               reads=["sbb"], writes=["gmx"])
        gmxb = gmx.rearrange("p (g u) -> p g u", u=1).broadcast_to((128, 4, 8))
        dep.op("v", V, lambda: V.tensor_tensor(m3, sb3, gmxb, ALU.is_ge),
               reads=["sbb", "gmx"], writes=["msk"])
        if "mskb" not in P.sb:
            P.sbuf("mskb", [128, 32], F32)
        mskb = P.sb["mskb"]
        mb3 = mskb.rearrange("p (g e) -> p g e", g=4)
        dep.op("v", V, lambda: V.scalar_tensor_tensor(mskb, msk, -1e30, sbb, ALU.mult, ALU.add),
               reads=["msk", "sbb"], writes=["mskb"])
        dep.op("v", V, lambda mb3=mb3: V.reduce_max(gmx2, mb3, axis=AX),
               reads=["mskb"], writes=["gmx2"])
        dep.op("v", V, lambda: V.tensor_tensor(gsum, gmx, gmx2, ALU.add),
               reads=["gmx", "gmx2"], writes=["gsum"])
        dep.op("v", V, lambda: V.reduce_max(gthr, gsum, axis=AX), reads=["gsum"], writes=["gthr"])
        dep.op("v", V, lambda: V.tensor_scalar(P.sb["gmx"], gsum, gthr, None, ALU.is_ge),
               reads=["gsum", "gthr"], writes=["gmx"], force_same=True)
        dep.op("v", V, lambda: V.scalar_tensor_tensor(P.sb["gmx2"], P.sb["gmx"], -1e30, gsum,
                                                      ALU.mult, ALU.add),
               reads=["gmx", "gsum"], writes=["gmx2"])
        dep.op("v", V, lambda: V.reduce_max(gthr, P.sb["gmx2"], axis=AX),
               reads=["gmx2"], writes=["gthr"])
        dep.op("v", V, lambda: V.tensor_scalar(gmx, gsum, gthr, None, ALU.is_ge),
               reads=["gsum", "gthr"], writes=["gmx"], force_same=True)
        gmb = gmx.rearrange("p (g u) -> p g u", u=1).broadcast_to((128, 4, 8))
        dep.op("v", V, lambda sb3=sb3, m3=m3, gmb=gmb: V.tensor_tensor(m3, sb3, gmb, ALU.mult),
               reads=["sbb", "gmx"], writes=["msk"])
        if "adj" not in P.sb:
            P.sbuf("adj", [128, 32], F32)
        adj3 = P.sb["adj"].rearrange("p (g e) -> p g e", g=4)
        dep.op("v", V, lambda gmb=gmb, adj3=adj3: V.tensor_scalar(adj3, gmb, 1e30, -1e30,
                                                                  ALU.mult, ALU.add),
               reads=["gmx"], writes=["adj"])
        dep.op("v", V, lambda: V.tensor_tensor(P.sb["mskb"], msk, P.sb["adj"], ALU.add),
               reads=["msk", "adj"], writes=["mskb"])
        dep.op("v", V, lambda: V.tensor_copy(msk, P.sb["mskb"]),
               reads=["mskb"], writes=["msk"])
        if "selm" not in P.sb:
            P.sbuf("selm", [128, 32], F32)
            P.sbuf("selacc", [128, 32], F32)
            P.sbuf("itcol", [128, 1], F32)
        dep.op("v", V, lambda: V.memset(P.sb["selacc"], 0.0), writes=["selacc"])
        if "selacc2" not in P.sb:
            P.sbuf("selacc2", [128, 32], F32)
        dep.op("v", V, lambda: V.memset(P.sb["selacc2"], 0.0), writes=["selacc2"])
        dep.fence(V, "v")
        mcur, moth = "msk", "mskb"
        acur, aoth = "selacc", "selacc2"
        for it8 in range(8):
            dep.op("v", V, lambda mcur=mcur: V.reduce_max(P.sb["itcol"], P.sb[mcur], axis=AX),
                   reads=[mcur], writes=["itcol"])
            dep.op("v", V, lambda mcur=mcur: V.tensor_scalar(P.sb["selm"], P.sb[mcur],
                                                             P.sb["itcol"], None, ALU.is_ge),
                   reads=[mcur, "itcol"], writes=["selm"], force_same=True)
            dep.op("v", V, lambda acur=acur, aoth=aoth: V.tensor_tensor(
                P.sb[aoth], P.sb[acur], P.sb["selm"], ALU.add),
                reads=[acur, "selm"], writes=[aoth])
            dep.op("v", V, lambda mcur=mcur, moth=moth: V.scalar_tensor_tensor(
                P.sb[moth], P.sb["selm"], -2e30, P.sb[mcur], ALU.mult, ALU.add),
                reads=["selm", mcur], writes=[moth])
            mcur, moth = moth, mcur
            acur, aoth = aoth, acur
        dep.op("v", V, lambda: V.tensor_tensor(cwsel, P.sb["selacc"], sco, ALU.mult),
               reads=["selacc", "sco"], writes=["cwsel"])
        dep.op("v", V, lambda: V.reduce_sum(wcol, cwsel, axis=AX), reads=["cwsel"],
               writes=["wcol"])
        dep.op("v", V, lambda: V.reciprocal(mcol, wcol), reads=["wcol"], writes=["mcol"],
               force_same=True)
        dep.op("v", V, lambda: V.tensor_scalar(mcol, mcol, 2.5, None, ALU.mult),
               reads=["mcol"], writes=["mcol"], force_same=True)
        dep.op("v", V, lambda: V.tensor_scalar(cwsel, cwsel, mcol, None, ALU.mult),
               reads=["cwsel", "mcol"], writes=["cwsel"], force_same=True)
        if "identf" not in P.sb:
            P.sbuf("identf", [128, 128], F32)
            idt = nc.inline_tensor(np.eye(128, dtype=np.float32), "id_ffn")
            dep.dma("y", SY, P.sb["identf"], idt.ap()[:, :], writes=["identf"])
        dep.op("t", TE, lambda ti=ti: TE.transpose(pt[0:32, 0:128], cwsel,
                                                   P.sb["identf"]),
               reads=["cwsel", "identf"], writes=["pt"])
        dep.op("v", V, lambda ti=ti: V.tensor_copy(cwt[:, 128 * ti:128 * (ti + 1)],
                                                   pt[0:32, 0:128]),
               reads=["pt"], writes=["cwt"])
    dep.dma("y", SY, dcwT[:, :], cwt, reads=["cwt"], writes=["cwT_d"])
    if dbg >= 1:
        dep.dma("y", SY, ddbg_cw[:, :], cwt, reads=["cwt"])
    dep.dma("g", GP, wb[:, 0:16 * 1024].rearrange("p (dt o) -> p dt o", dt=16),
            dfc1T.rearrange("(dt p) o -> p dt o", p=128), writes=["wb"])
    for ot in range(8):
        for tc in range(2):
            def mmx(ot=ot, tc=tc):
                last = None
                for dt in range(16):
                    last = TE.matmul(pa[:, 0:512],
                                     wb[:, dt * 1024 + ot * 128: dt * 1024 + ot * 128 + 128],
                                     xn[:, dt * TPC + tc * 512: dt * TPC + tc * 512 + 512],
                                     start=(dt == 0), stop=(dt == 15))
                return last
            dep.op("t", TE, mmx, reads=["wb", "xn"], writes=["pa"])
            dep.op("v", V, lambda ot=ot, tc=tc: V.tensor_copy(
                xlat[:, ot * TPC + tc * 512: ot * TPC + tc * 512 + 512], pa[:, 0:512]),
                reads=["pa"], writes=["xlat"])
            if dbg >= 2:
                dep.op("v", V, lambda: V.tensor_copy(rscr, pa[:, 0:512]),
                       reads=["pa"], writes=["rscr"])
                dep.dma("y", SY, ddbg_xlat[ot * 128:(ot + 1) * 128, tc * 512:(tc + 1) * 512],
                        rscr, reads=["rscr"])
    cwb = P.sbuf("cwb", [128, 512], F32)
    for tc in range(2):
        ycur, yoth = ("yaccA", "yaccB")
        dep.op("v", V, lambda: V.memset(P.sb["yaccA"], 0.0), writes=["yaccA"])
        dep.fence(V, "v")
        for e in range(32):
            dep.dma("y", SY, cwrow, dcwT[e:e + 1, tc * 512:(tc + 1) * 512],
                    reads=["cwT_d"], writes=["cwrow"])
            dep.op("t", TE, lambda: TE.matmul(
                pb[:, 0:512], ones1, cwrow,
                start=True, stop=True), reads=["cwrow", "ones1"], writes=["pb"])
            dep.op("v", V, lambda: V.tensor_copy(cwb, pb[:, 0:512]), reads=["pb"], writes=["cwb"])
            dep.dma("g", GP, wb[:, 0:8 * 1024].rearrange("p (dt o) -> p dt o", dt=8),
                    dw1T[e].rearrange("(dt p) o -> p dt o", p=128), writes=["wb"])
            for it in range(8):
                def mmh(e=e, it=it, tc=tc):
                    last = None
                    for lt in range(8):
                        last = TE.matmul(pa[:, 0:512],
                                         wb[:, lt * 1024 + it * 128: lt * 1024 + it * 128 + 128],
                                         xlat[:, lt * TPC + tc * 512: lt * TPC + tc * 512 + 512],
                                         start=(lt == 0), stop=(lt == 7))
                    return last
                dep.op("t", TE, mmh, reads=["wb", "xlat"], writes=["pa"])
                dep.op("v", V, lambda: V.tensor_scalar(rscr, pa[:, 0:512], 0.0, None, ALU.max),
                       reads=["pa"], writes=["rscr"])
                dep.op("v", V, lambda: V.scalar_tensor_tensor(rscr2, rscr, 1.0, rscr,
                                                              ALU.mult, ALU.mult),
                       reads=["rscr"], writes=["rscr2"])
                dep.op("v", V, lambda it=it: V.tensor_tensor(hbuf[:, it * 512:(it + 1) * 512],
                                                             rscr2, cwb, ALU.mult),
                       reads=["rscr2", "cwb"], writes=["hbuf"])
            dep.dma("g", GP, wb[:, 0:8 * 1024].rearrange("p (dt o) -> p dt o", dt=8),
                    dw2T[e].rearrange("(dt p) o -> p dt o", p=128), writes=["wb"])
            for lt in range(8):
                def mmy(e=e, lt=lt):
                    last = None
                    for it in range(8):
                        last = TE.matmul(pa[:, 0:512],
                                         wb[:, it * 1024 + lt * 128: it * 1024 + lt * 128 + 128],
                                         hbuf[:, it * 512:(it + 1) * 512],
                                         start=(it == 0), stop=(it == 7))
                    return last
                dep.op("t", TE, mmy, reads=["wb", "hbuf"], writes=["pa"])
                dep.op("v", V, lambda lt=lt, ycur=ycur, yoth=yoth: V.tensor_tensor(
                    P.sb[yoth][:, lt * 512:(lt + 1) * 512],
                    P.sb[ycur][:, lt * 512:(lt + 1) * 512], pa[:, 0:512], ALU.add),
                    reads=[ycur, "pa"], writes=[yoth])
            ycur, yoth = yoth, ycur
        if dbg >= 3:
            for lt in range(8):
                dep.dma("y", SY, ddbg_ylat[lt * 128:(lt + 1) * 128, tc * 512:(tc + 1) * 512],
                        P.sb[ycur][:, lt * 512:(lt + 1) * 512], reads=[ycur])
        for lt in range(8):
            dep.op("v", V, lambda lt=lt, ycur=ycur: V.tensor_copy(
                ylbf[:, lt * 512: lt * 512 + 512],
                P.sb[ycur][:, lt * 512:(lt + 1) * 512]),
                reads=[ycur], writes=["ylbf"])
        dep.dma("g", GP, wb[:, 0:16 * 1024].rearrange("p (dt o) -> p dt o", dt=16),
                dsuT[:, 0:1024].rearrange("(dt p) o -> p dt o", p=128), writes=["wb"])
        for og in range(4):
            if og > 0:
                dep.dma("g", GP, wb[:, 0:16 * 1024].rearrange("p (dt o) -> p dt o", dt=16),
                        dsuT[:, 1024 * og:1024 * (og + 1)].rearrange("(dt p) o -> p dt o", p=128),
                        writes=["wb"])
            for ot in range(8):
                def mmsh(og=og, ot=ot, tc=tc):
                    last = None
                    for dt in range(16):
                        last = TE.matmul(pa[:, 0:512],
                                         wb[:, dt * 1024 + ot * 128: dt * 1024 + ot * 128 + 128],
                                         xn[:, dt * TPC + tc * 512: dt * TPC + tc * 512 + 512],
                                         start=(dt == 0), stop=(dt == 15))
                    return last
                dep.op("t", TE, mmsh, reads=["wb", "xn"], writes=["pa"])
                dep.op("v", V, lambda: V.tensor_scalar(rscr, pa[:, 0:512], 0.0, None, ALU.max),
                       reads=["pa"], writes=["rscr"])
                dep.op("v", V, lambda og=og, ot=ot: V.scalar_tensor_tensor(
                    shs[:, (og * 8 + ot) * 512:(og * 8 + ot) * 512 + 512], rscr, 1.0, rscr,
                    ALU.mult, ALU.mult),
                    reads=["rscr"], writes=["shs"])
        dep.dma("g", GP, onc.rearrange("p (dt t) -> p dt t", dt=16),
                donT.rearrange("(dt p) t -> p dt t", p=128)[:, :, tc * 512:(tc + 1) * 512],
                writes=["onc"])
        for ot in range(16):
            dep.dma("g", GP, wb[:, 0:32 * 128].rearrange("p (dt o) -> p dt o", dt=32),
                    dsdT[:, ot * 128:(ot + 1) * 128].rearrange("(dt p) o -> p dt o", p=128),
                    writes=["wb"])
            dep.op("t", TE, lambda ot=ot, tc=tc: (
                [TE.matmul(pa[:, 0:512], wb[:, dt * 128:dt * 128 + 128],
                           shs[:, dt * 512:(dt + 1) * 512], start=(dt == 0), stop=False)
                 for dt in range(32)][-1]), reads=["wb", "shs"], writes=["pa_open"])
            if "wb2" not in P.sb:
                P.sbuf("wb2", [128, 8 * 128], BF16)
            dep.dma("g", GP, P.sb["wb2"].rearrange("p (dt o) -> p dt o", dt=8),
                    dfc2T[:, ot * 128:(ot + 1) * 128].rearrange("(dt p) o -> p dt o", p=128),
                    writes=["wb2"])
            dep.op("t", TE, lambda ot=ot, tc=tc: (
                [TE.matmul(pa[:, 0:512], P.sb["wb2"][:, dt * 128:dt * 128 + 128],
                           ylbf[:, dt * 512: dt * 512 + 512],
                           start=False, stop=(dt == 7)) for dt in range(8)][-1]),
                reads=["wb2", "ylbf", "pa_open"], writes=["pa"])
            if dbg >= 4:
                dep.op("v", V, lambda: V.tensor_copy(rscr, pa[:, 0:512]),
                       reads=["pa"], writes=["rscr"])
                dep.dma("y", SY, ddbg_g0s[ot * 128:(ot + 1) * 128, tc * 512:(tc + 1) * 512],
                        rscr, reads=["rscr"])
            dep.op("s", SC, lambda: SC.activation(g0s, pa[:, 0:512], AF.Silu),
                   reads=["pa"], writes=["g0s"])
            if "wb3" not in P.sb:
                P.sbuf("wb3", [128, 16 * 128], BF16)
            dep.dma("g", GP, P.sb["wb3"].rearrange("p (dt o) -> p dt o", dt=16),
                    dwo0[:, ot * 128:(ot + 1) * 128].rearrange("(dt p) o -> p dt o", p=128),
                    writes=["wb3"])
            def mms0(ot=ot, tc=tc):
                last = None
                for dt in range(16):
                    last = TE.matmul(pb[:, 0:512], P.sb["wb3"][:, dt * 128:dt * 128 + 128],
                                     onc[:, dt * 512:(dt + 1) * 512],
                                     start=(dt == 0), stop=(dt == 15))
                return last
            dep.op("t", TE, mms0, reads=["wb3", "onc"], writes=["pb"])
            if dbg >= 4:
                dep.op("v", V, lambda: V.tensor_copy(rscr, pb[:, 0:512]),
                       reads=["pb"], writes=["rscr"])
                dep.dma("y", SY, ddbg_s0[ot * 128:(ot + 1) * 128, tc * 512:(tc + 1) * 512],
                        rscr, reads=["rscr"])
            dep.dma("y", SY, xtile, dxT[ot * 128:(ot + 1) * 128, tc * 512:(tc + 1) * 512],
                    writes=["xtile"])
            dep.op("v", V, lambda: V.tensor_tensor(rscr, pb[:, 0:512], g0s, ALU.mult),
                   reads=["pb", "g0s"], writes=["rscr"])
            dep.op("v", V, lambda ot=ot, tc=tc: V.tensor_tensor(
                rscr2, rscr, xtile, ALU.add),
                reads=["rscr", "xtile"], writes=["rscr2"])
            dep.dma("y", SY, dx1T[ot * 128:(ot + 1) * 128, tc * 512:(tc + 1) * 512],
                    rscr2, reads=["rscr2"])
    return P.finish()


KERNEL_PART2 = True


def build_mha():
    P = Prog("mha")
    nc = P.nc
    dxn1 = nc.dram_tensor("xn1T", [D, T], F32, kind="ExternalInput").ap()
    dwqT = nc.dram_tensor("wqT", [D, 1024], F32, kind="ExternalInput").ap()
    dwkT = nc.dram_tensor("wkT", [D, 1024], F32, kind="ExternalInput").ap()
    dwvT = nc.dram_tensor("wvT", [D, 1024], F32, kind="ExternalInput").ap()
    dbq = nc.dram_tensor("bq", [128, 8], F32, kind="ExternalInput").ap()
    dbk = nc.dram_tensor("bk", [128, 8], F32, kind="ExternalInput").ap()
    dbv = nc.dram_tensor("bv", [128, 8], F32, kind="ExternalInput").ap()
    don1 = nc.dram_tensor("on1T", [1024, T], F32, kind="ExternalOutput").ap()
    idbf = nc.inline_tensor(np.eye(128, dtype=np.float32).astype(BF), "id_mha")
    dm = np.where(np.arange(128)[:, None] >= np.arange(128)[None, :], 0.0, -1e30)
    dmask = nc.inline_tensor(dm.astype(np.float32), "dmask_mha")
    P.start()
    eng = P.engines()
    V, SC, TE, SY, GP = (eng[k] for k in "vstyg")
    dep = P.dep
    xn1 = P.sbuf("xn1", [128, 16 * T], BF16)
    wqb = P.sbuf("wqb", [128, 16 * 128], BF16)
    qs = P.sbuf("qs", [128, T], F32)
    ks = P.sbuf("ks", [128, T], F32)
    tmp = P.sbuf("tmp", [128, T], F32)
    tmp2 = P.sbuf("tmp2", [128, T], F32)
    qr = P.sbuf("qr", [128, T], BF16)
    qsw = P.sbuf("qsw", [128, T], F32)
    kr = P.sbuf("kr", [128, T], BF16)
    vbf = P.sbuf("vbf", [128, T], BF16)
    vtok = P.sbuf("vtok", [128, 16 * 128], BF16)
    cosb = P.sbuf("cosb", [128, T], F32)
    sinn = P.sbuf("sinn", [128, T], F32)
    srow = P.sbuf("srow", [128, T], F32)
    pslab = P.sbuf("pslab", [128, T], F32)
    pbf = P.sbuf("pbf", [128, T], BF16)
    ptb = P.sbuf("ptb", [128, 16 * 128], BF16)
    oslab = P.sbuf("oslab", [128, T], F32)
    bqs = P.sbuf("bqs", [128, 8], F32)
    bks = P.sbuf("bks", [128, 8], F32)
    bvs = P.sbuf("bvs", [128, 8], F32)
    rmax = P.sbuf("rmax", [128, 1], F32)
    rsum = P.sbuf("rsum", [128, 1], F32)
    rrec = P.sbuf("rrec", [128, 1], F32)
    idb = P.sbuf("idb", [128, 128], BF16)
    dmk = P.sbuf("dmk", [128, 128], F32)
    throw_ = P.sbuf("throw", [1, 128], F32)
    thr2 = P.sbuf("thr2", [1, 128], F32)
    angi = P.sbuf("angi", [128, 512], I32)
    angf = P.sbuf("angf", [128, 512], F32)
    posr = P.sbuf("posr", [1, T], F32)
    pa = P.psum("pa", [128, 512], F32)
    pb = P.psum("pb", [128, 512], F32)
    ptr = P.psum("ptr", [128, 1024], BF16)
    pv = P.psum("pv", [128, 128], F32)
    pang = P.psum("pang", [128, 512], F32)

    dep.dma("g", GP, xn1.rearrange("p (dt t) -> p dt t", dt=16),
            dxn1.rearrange("(dt p) t -> p dt t", p=128), writes=["xn1"])
    dep.dma("y", SY, idb, idbf.ap()[:, :], writes=["idb"])
    dep.dma("y", SY, dmk, dmask.ap()[:, :], writes=["dmk"])
    dep.dma("y", SY, bqs, dbq[:, :], writes=["bqs"])
    dep.dma("y", SY, bks, dbk[:, :], writes=["bks"])
    dep.dma("y", SY, bvs, dbv[:, :], writes=["bvs"])
    dep.op("g", GP, lambda: GP.iota(throw_, pattern=[[1, 128]], base=0, channel_multiplier=0,
                                    allow_small_or_imprecise_dtypes=True), writes=["throw"])
    dep.op("g", GP, lambda: GP.iota(posr, pattern=[[1, T]], base=1, channel_multiplier=0,
                                    allow_small_or_imprecise_dtypes=True), writes=["posr"])
    dep.op("v", V, lambda: V.tensor_scalar(thr2, throw_, 64.0, None, ALU.is_ge),
           reads=["throw"], writes=["thr2"])
    dep.op("v", V, lambda: V.scalar_tensor_tensor(throw_, thr2, -64.0, throw_, ALU.mult, ALU.add),
           reads=["thr2", "throw"], writes=["throw"])
    dep.op("s", SC, lambda: SC.activation(throw_, throw_, AF.Exp, scale=LN2),
           reads=["throw"], writes=["throw"])
    dep.op("s", SC, lambda: SC.activation(throw_, throw_, AF.Exp, scale=-LN1E4 / 128.0),
           reads=["throw"], writes=["throw"], force_same=True)
    for ch in range(4):
        dep.op("t", TE, lambda ch=ch: TE.matmul(pang[:, 0:512], throw_,
                                                posr[:, 512 * ch:512 * (ch + 1)],
                                                start=True, stop=True),
               reads=["throw", "posr"], writes=["pang"])
        sl = slice(512 * ch, 512 * (ch + 1))
        dep.op("v", V, lambda: V.tensor_scalar(angi, pang[:, 0:512], 1.0 / (2 * PI), None,
                                               ALU.mult),
               reads=["pang"], writes=["angi"])
        dep.op("v", V, lambda: V.tensor_copy(angf, angi), reads=["angi"], writes=["angf"])
        dep.op("v", V, lambda sl=sl: V.scalar_tensor_tensor(tmp[:, sl], angf, -2 * PI,
                                                            pang[:, 0:512], ALU.mult, ALU.add),
               reads=["angf", "pang"], writes=["tmp"])
        dep.op("s", SC, lambda sl=sl: SC.activation(sinn[:, sl], tmp[:, sl], AF.Sin),
               reads=["tmp"], writes=["sinn"])
        dep.op("v", V, lambda sl=sl: V.tensor_scalar(tmp2[:, sl], pang[:, 0:512], PI / 2, None,
                                                     ALU.add),
               reads=["pang"], writes=["tmp2"])
        dep.op("v", V, lambda sl=sl: V.tensor_scalar(angi, tmp2[:, sl], 1.0 / (2 * PI), None,
                                                     ALU.mult),
               reads=["tmp2"], writes=["angi"])
        dep.op("v", V, lambda: V.tensor_copy(angf, angi), reads=["angi"], writes=["angf"])
        dep.op("v", V, lambda sl=sl: V.scalar_tensor_tensor(tmp[:, sl], angf, -2 * PI,
                                                            tmp2[:, sl], ALU.mult, ALU.add),
               reads=["angf", "tmp2"], writes=["tmp"])
        dep.op("s", SC, lambda sl=sl: SC.activation(cosb[:, sl], tmp[:, sl], AF.Sin),
               reads=["tmp"], writes=["cosb"])
    dep.op("v", V, lambda: V.tensor_scalar(sinn[0:64, :], sinn[0:64, :], -1.0, None, ALU.mult),
           reads=["sinn"], writes=["sinn"], force_same=True)

    def proj_head(dw, bkey, bias, outbuf, outkey):
        dep.dma("g", GP, wqb.rearrange("p (dt o) -> p dt o", dt=16),
                dw.rearrange("(dt p) o -> p dt o", p=128), writes=["wqb"])
        for tc in range(4):
            def mmp(tc=tc):
                last = None
                for dt in range(16):
                    last = TE.matmul(pa[:, 0:512], wqb[:, dt * 128: dt * 128 + 128],
                                     xn1[:, dt * T + tc * 512: dt * T + tc * 512 + 512],
                                     start=(dt == 0), stop=(dt == 15))
                return last
            dep.op("t", TE, mmp, reads=["wqb", "xn1"], writes=["pa"])
            dep.op("v", V, lambda tc=tc: V.tensor_scalar(outbuf[:, 512 * tc:512 * (tc + 1)],
                                                         pa[:, 0:512], bias, None, ALU.add),
                   reads=["pa", bkey], writes=[outkey])

    def rope_apply(inb, inkey, outb, outkey):
        dep.dma("g", GP, qsw[0:64, :], inb[64:128, :], reads=[inkey], writes=["qsw"])
        dep.dma("g", GP, qsw[64:128, :], inb[0:64, :], reads=[inkey], writes=["qsw"])
        dep.op("v", V, lambda: V.tensor_tensor(tmp, inb, cosb, ALU.mult),
               reads=[inkey, "cosb"], writes=["tmp"])
        dep.op("v", V, lambda: V.tensor_tensor(tmp2, qsw, sinn, ALU.mult),
               reads=["qsw", "sinn"], writes=["tmp2"])
        dep.op("v", V, lambda: V.tensor_tensor(outb, tmp, tmp2, ALU.add),
               reads=["tmp", "tmp2"], writes=[outkey])

    for h in range(HPC):
        proj_head(dwqT[:, 128 * h:128 * (h + 1)], "bqs", bqs[:, h:h + 1], qs, "qs")
        proj_head(dwkT[:, 128 * h:128 * (h + 1)], "bks", bks[:, h:h + 1], ks, "ks")
        rope_apply(qs, "qs", qr, "qr")
        rope_apply(ks, "ks", kr, "kr")
        proj_head(dwvT[:, 128 * h:128 * (h + 1)], "bvs", bvs[:, h:h + 1], tmp, "tmp")
        dep.op("v", V, lambda: V.tensor_copy(vbf, tmp), reads=["tmp"], writes=["vbf"])
        for st in range(16):
            dep.op("t", TE, lambda st=st: TE.transpose(ptr[0:128, 0:128],
                                                       vbf[:, 128 * st:128 * (st + 1)], idb),
                   reads=["vbf", "idb"], writes=["ptr"])
            dep.op("v", V, lambda st=st: V.tensor_copy(vtok[:, 128 * st:128 * (st + 1)],
                                                       ptr[0:128, 0:128]),
                   reads=["ptr"], writes=["vtok"])
        for ti in range(16):
            lim = (ti + 1) * 128
            nch = (lim + 511) // 512
            for ch in range(nch):
                w = min(512, lim - ch * 512)
                def mmS(ti=ti, ch=ch, w=w):
                    return TE.matmul(pa[:, 0:w], qr[:, 128 * ti:128 * (ti + 1)],
                                     kr[:, 512 * ch:512 * ch + w], start=True, stop=True)
                dep.op("t", TE, mmS, reads=["qr", "kr"], writes=["pa"])
                dep.op("v", V, lambda ch=ch, w=w: V.tensor_scalar(
                    srow[:, 512 * ch:512 * ch + w], pa[:, 0:w], DSCALE, None, ALU.mult),
                    reads=["pa"], writes=["srow"])
            dep.op("v", V, lambda ti=ti: V.tensor_tensor(srow[:, 128 * ti:128 * (ti + 1)],
                                                         srow[:, 128 * ti:128 * (ti + 1)],
                                                         dmk, ALU.add),
                   reads=["srow", "dmk"], writes=["srow"])
            dep.op("v", V, lambda lim=lim: V.reduce_max(rmax, srow[:, 0:lim], axis=AX),
                   reads=["srow"], writes=["rmax"])
            dep.op("v", V, lambda: V.tensor_scalar(rmax, rmax, -1.0, None, ALU.mult),
                   reads=["rmax"], writes=["rmax"], force_same=True)
            dep.op("s", SC, lambda lim=lim: SC.activation(pslab[:, 0:lim], srow[:, 0:lim],
                                                          AF.Exp, bias=rmax,
                                                          accum_out=rsum),
                   reads=["srow", "rmax"], writes=["pslab", "rsum"])
            dep.op("v", V, lambda: V.reciprocal(rrec, rsum), reads=["rsum"], writes=["rrec"])
            dep.op("v", V, lambda lim=lim: V.tensor_scalar(pbf[:, 0:lim], pslab[:, 0:lim],
                                                           rrec, None, ALU.mult),
                   reads=["pslab", "rrec"], writes=["pbf"])
            for st in range(ti + 1):
                dep.op("t", TE, lambda st=st: TE.transpose(ptr[0:128, 0:128],
                                                           pbf[:, 128 * st:128 * (st + 1)], idb),
                       reads=["pbf", "idb"], writes=["ptr"])
                dep.op("v", V, lambda st=st: V.tensor_copy(ptb[:, 128 * st:128 * (st + 1)],
                                                           ptr[0:128, 0:128]),
                       reads=["ptr"], writes=["ptb"])
            def mmPV(ti=ti):
                last = None
                for st in range(ti + 1):
                    last = TE.matmul(pv[:, 0:128], vtok[:, 128 * st:128 * (st + 1)],
                                     ptb[:, 128 * st:128 * (st + 1)],
                                     start=(st == 0), stop=(st == ti))
                return last
            dep.op("t", TE, mmPV, reads=["vtok", "ptb"], writes=["pv"])
            dep.op("v", V, lambda ti=ti: V.tensor_copy(oslab[:, 128 * ti:128 * (ti + 1)],
                                                       pv[:, 0:128]),
                   reads=["pv"], writes=["oslab"])
        dep.dma("y", SY, don1[128 * h:128 * (h + 1), :], oslab, reads=["oslab"])
    return P.finish()


def build_bl1():
    P = Prog("bl1")
    nc = P.nc
    dqa = nc.dram_tensor("qaT", [D, TPC], F32, kind="ExternalInput").ap()
    dwgv = nc.dram_tensor("wgvT", [D, 16384], F32, kind="ExternalInput").ap()
    dds0 = nc.dram_tensor("ds0r", [1, TPC], F32, kind="ExternalInput").ap()
    dht = nc.dram_tensor("htilT", [8192, TPC], F32, kind="ExternalOutput").ap()
    dsgv = nc.dram_tensor("sgv", [1, 1], F32, kind="ExternalOutput").ap()
    P.start()
    eng = P.engines()
    V, SC, TE, SY, GP = (eng[k] for k in "vstyg")
    dep = P.dep
    qa = P.sbuf("qa", [128, 16 * TPC], BF16)
    wsl = P.sbuf("wsl", [128, 16 * 512], BF16)
    wslg = P.sbuf("wslg", [128, 16 * 512], BF16)
    wq1 = P.sbuf("wq1", [128, 16 * 512], BF16)
    wq2 = P.sbuf("wq2", [128, 16 * 512], BF16)
    racc = P.sbuf("racc", [128, 512], F32)
    rcol = P.sbuf("rcol", [128, 1], F32)
    sgvc = P.sbuf("sgvc", [128, 1], F32)
    rsc = P.sbuf("rsc", [128, 1], F32)
    ds0s = P.sbuf("ds0s", [1, TPC], F32)
    cb = P.sbuf("cb", [128, TPC], F32)
    t1 = P.sbuf("t1", [128, 512], F32)
    t2 = P.sbuf("t2", [128, 512], F32)
    t3 = P.sbuf("t3", [128, 512], F32)
    qi = P.sbuf("qi", [128, 16 * 512], I32)
    ones1 = P.sbuf("ones1", [1, 128], F32)
    onesc = P.sbuf("onesc", [128, 1], F32)
    pa = P.psum("pa", [128, 512], F32)
    pb = P.psum("pb", [128, 512], F32)
    pr = P.psum("pr", [1, 512], F32)
    dep.dma("g", GP, qa.rearrange("p (dt t) -> p dt t", dt=16),
            dqa.rearrange("(dt p) t -> p dt t", p=128), writes=["qa"])
    dep.dma("y", SY, ds0s, dds0[:, :], writes=["ds0s"])
    dep.op("v", V, lambda: V.memset(ones1, 1.0), writes=["ones1"])
    dep.op("v", V, lambda: V.memset(onesc, 1.0), writes=["onesc"])
    dep.op("v", V, lambda: V.memset(racc, 0.0), writes=["racc"])
    dep.fence(V, "v")
    for g8 in range(32):
        dep.dma("g", GP, wsl.rearrange("p (dt o) -> p dt o", dt=16),
                dwgv[:, 512 * g8:512 * (g8 + 1)].rearrange("(dt p) o -> p dt o", p=128),
                writes=["wsl"])
        dep.op("v", V, lambda g8=g8: V.reduce_sum(racc[:, g8:g8 + 1], wsl, axis=AX,
                                                  apply_absolute_value=True),
               reads=["wsl"], writes=["racc"])
    dep.op("v", V, lambda: V.reduce_sum(rcol, racc[:, 0:32], axis=AX),
           reads=["racc"], writes=["rcol"])
    dep.op("t", TE, lambda: TE.matmul(pr[:, 0:1], rcol, onesc, start=True, stop=True),
           reads=["rcol", "onesc"], writes=["pr"])
    dep.op("v", V, lambda: V.tensor_scalar(pr[0:1, 0:1], pr[0:1, 0:1], 1.0 / (D * 16384.0),
                                           None, ALU.mult),
           reads=["pr"], writes=["pr"], force_same=True)
    dep.op("v", V, lambda: V.tensor_scalar(pr[0:1, 0:1], pr[0:1, 0:1], 1e-5, None, ALU.max),
           reads=["pr"], writes=["pr"], force_same=True)
    P.sbuf("sgvr", [1, 1], F32)
    dep.op("s", SC, lambda: SC.copy(P.sb["sgvr"], pr[0:1, 0:1]), reads=["pr"], writes=["sgvr"])
    dep.dma("y", SY, dsgv[:, :], P.sb["sgvr"], reads=["sgvr"])
    dep.op("t", TE, lambda: TE.matmul(pb[:, 0:1], ones1, P.sb["sgvr"], start=True, stop=True),
           reads=["sgvr", "ones1"], writes=["pb"])
    dep.op("v", V, lambda: V.tensor_copy(sgvc, pb[:, 0:1]), reads=["pb"], writes=["sgvc"])
    dep.op("v", V, lambda: V.reciprocal(rsc, sgvc), reads=["sgvc"], writes=["rsc"])
    for a in range(TPC // 512):
        dep.op("t", TE, lambda a=a: TE.matmul(pb[:, 0:512], ones1,
                                              ds0s[:, 512 * a:512 * (a + 1)],
                                              start=True, stop=True),
               reads=["ds0s", "ones1"], writes=["pb"])
        dep.op("v", V, lambda a=a: V.tensor_scalar(cb[:, 512 * a:512 * (a + 1)], pb[:, 0:512],
                                                   sgvc, None, ALU.mult),
               reads=["pb", "sgvc"], writes=["cb"])
    for g8 in range(16):
        dep.dma("g", GP, wsl.rearrange("p (dt o) -> p dt o", dt=16),
                dwgv[:, 512 * g8:512 * (g8 + 1)].rearrange("(dt p) o -> p dt o", p=128),
                writes=["wsl"])
        dep.dma("g", GP, wslg.rearrange("p (dt o) -> p dt o", dt=16),
                dwgv[:, 8192 + 512 * g8:8192 + 512 * (g8 + 1)].rearrange("(dt p) o -> p dt o", p=128),
                writes=["wslg"])
        for (src, dst, sk, dk_) in [("wsl", "wq1", "wsl", "wq1"), ("wslg", "wq2", "wslg", "wq2")]:
            dep.op("v", V, lambda src=src: V.tensor_scalar(qi, P.sb[src], rsc, None, ALU.mult),
                   reads=[src, "rsc"], writes=["qi"])
            dep.op("v", V, lambda dst=dst: V.tensor_scalar(P.sb[dst], qi, -1.0, 1.0,
                                                           ALU.max, ALU.min),
                   reads=["qi"], writes=[dst])
        for osub in range(4):
            for tc in range(TPC // 512):
                def mmv(g8=g8, osub=osub, tc=tc):
                    last = None
                    for dt in range(16):
                        last = TE.matmul(pa[:, 0:512],
                                         wq1[:, dt * 512 + osub * 128: dt * 512 + osub * 128 + 128],
                                         qa[:, dt * TPC + tc * 512: dt * TPC + tc * 512 + 512],
                                         start=(dt == 0), stop=(dt == 15))
                    return last
                dep.op("t", TE, mmv, reads=["wq1", "qa"], writes=["pa"])
                def mmg(g8=g8, osub=osub, tc=tc):
                    last = None
                    for dt in range(16):
                        last = TE.matmul(pb[:, 0:512],
                                         wq2[:, dt * 512 + osub * 128: dt * 512 + osub * 128 + 128],
                                         qa[:, dt * TPC + tc * 512: dt * TPC + tc * 512 + 512],
                                         start=(dt == 0), stop=(dt == 15))
                    return last
                dep.op("t", TE, mmg, reads=["wq2", "qa"], writes=["pb"])
                dep.op("v", V, lambda tc=tc: V.tensor_tensor(t1, pb[:, 0:512],
                                                             cb[:, tc * 512:(tc + 1) * 512],
                                                             ALU.mult),
                       reads=["pb", "cb"], writes=["t1"])
                dep.op("s", SC, lambda: SC.activation(t2, t1, AF.Silu), reads=["t1"],
                       writes=["t2"])
                dep.op("v", V, lambda: V.tensor_tensor(t3, pa[:, 0:512], t2, ALU.mult),
                       reads=["pa", "t2"], writes=["t3"])
                dep.dma("y", SY, dht[512 * g8 + 128 * osub:512 * g8 + 128 * (osub + 1),
                                     512 * tc:512 * (tc + 1)], t3, reads=["t3"])
    return P.finish()


def build_quant2():
    P = Prog("quant2")
    nc = P.nc
    dh = nc.dram_tensor("htil", [TPC, 8192], F32, kind="ExternalInput").ap()
    dds0 = nc.dram_tensor("ds0", [128, 8], F32, kind="ExternalInput").ap()
    dsgv = nc.dram_tensor("sgv", [1, 1], F32, kind="ExternalInput").ap()
    dqa2 = nc.dram_tensor("qa2", [TPC, 8192], F32, kind="ExternalOutput").ap()
    dds2 = nc.dram_tensor("ds2", [128, 8], F32, kind="ExternalOutput").ap()
    P.start()
    eng = P.engines()
    V, SC, TE, SY, GP = (eng[k] for k in "vstyg")
    dep = P.dep
    ht = P.sbuf("ht", [128, 8192], F32)
    qi = P.sbuf("qi", [128, 8192], I32)
    qf = P.sbuf("qf", [128, 8192], F32)
    mx = P.sbuf("mx", [128, 1], F32)
    ccol = P.sbuf("ccol", [128, 1], F32)
    scol = P.sbuf("scol", [128, 1], F32)
    sgvc = P.sbuf("sgvc", [128, 1], F32)
    sgvr = P.sbuf("sgvr", [1, 1], F32)
    ds0s = P.sbuf("ds0s", [128, 8], F32)
    ds2s = P.sbuf("ds2s", [128, 8], F32)
    ones1 = P.sbuf("ones1", [1, 128], F32)
    pb = P.psum("pb", [128, 512], F32)
    dep.dma("y", SY, ds0s, dds0[:, :], writes=["ds0s"])
    dep.dma("y", SY, sgvr, dsgv[:, :], writes=["sgvr"])
    dep.op("v", V, lambda: V.memset(ones1, 1.0), writes=["ones1"])
    dep.fence(V, "v")
    dep.op("t", TE, lambda: TE.matmul(pb[:, 0:1], ones1, sgvr, start=True, stop=True),
           reads=["sgvr", "ones1"], writes=["pb"])
    dep.op("v", V, lambda: V.tensor_copy(sgvc, pb[:, 0:1]), reads=["pb"], writes=["sgvc"])
    for ti in range(8):
        rs = slice(128 * ti, 128 * (ti + 1))
        dep.dma("y", SY, ht, dh[rs, :], writes=["ht"])
        dep.op("v", V, lambda ti=ti: V.tensor_scalar(ccol, sgvc, ds0s[:, ti:ti + 1], None,
                                                     ALU.mult),
               reads=["sgvc", "ds0s"], writes=["ccol"])
        dep.op("v", V, lambda: V.reduce_max(mx, ht, axis=AX, apply_absolute_value=True),
               reads=["ht"], writes=["mx"])
        dep.op("v", V, lambda: V.tensor_scalar(mx, mx, ccol, None, ALU.mult),
               reads=["mx", "ccol"], writes=["mx"], force_same=True)
        dep.op("v", V, lambda: V.tensor_scalar(mx, mx, 1e-5, None, ALU.max),
               reads=["mx"], writes=["mx"], force_same=True)
        dep.op("v", V, lambda ti=ti: V.tensor_scalar(ds2s[:, ti:ti + 1], mx, 1.0 / 127.0,
                                                     None, ALU.mult),
               reads=["mx"], writes=["ds2s"], force_same=True)
        dep.op("v", V, lambda: V.reciprocal(scol, mx), reads=["mx"], writes=["scol"],
               force_same=True)
        dep.op("v", V, lambda: V.tensor_scalar(scol, scol, 127.0, None, ALU.mult),
               reads=["scol"], writes=["scol"], force_same=True)
        dep.op("v", V, lambda: V.tensor_scalar(scol, scol, ccol, None, ALU.mult),
               reads=["scol", "ccol"], writes=["scol"], force_same=True)
        dep.op("v", V, lambda: V.tensor_scalar(qi, ht, scol, None, ALU.mult),
               reads=["ht", "scol"], writes=["qi"], force_same=True)
        dep.op("v", V, lambda: V.tensor_copy(qf, qi), reads=["qi"], writes=["qf"])
        dep.dma("y", SY, dqa2[rs, :], qf, reads=["qf"])
    dep.dma("y", SY, dds2[:, :], ds2s, reads=["ds2s"])
    return P.finish()


def build_bl2():
    P = Prog("bl2")
    nc = P.nc
    dqa2 = nc.dram_tensor("qa2T", [8192, TPC], F32, kind="ExternalInput").ap()
    dwout = nc.dram_tensor("woutT", [8192, D], F32, kind="ExternalInput").ap()
    dds2 = nc.dram_tensor("ds2r", [1, TPC], F32, kind="ExternalInput").ap()
    dxn1 = nc.dram_tensor("xn1T", [D, TPC], F32, kind="ExternalInput").ap()
    dx1 = nc.dram_tensor("x1T", [D, TPC], F32, kind="ExternalInput").ap()
    don1 = nc.dram_tensor("on1T", [D, TPC], F32, kind="ExternalInput").ap()
    dwoT = nc.dram_tensor("woT", [D, D], F32, kind="ExternalInput").ap()
    dbo = nc.dram_tensor("bo", [128, 16], F32, kind="ExternalInput").ap()
    dout = nc.dram_tensor("outT", [D, TPC], F32, kind="ExternalOutput").ap()
    P.start()
    eng = P.engines()
    V, SC, TE, SY, GP = (eng[k] for k in "vstyg")
    dep = P.dep
    qa2c = P.sbuf("qa2c", [128, 64 * 512], BF16)
    wsl = P.sbuf("wsl", [128, 8 * 512], BF16)
    wq1 = P.sbuf("wq1", [128, 64 * 512], BF16)
    qi = P.sbuf("qi", [128, 8 * 512], I32)
    won = P.sbuf("won", [128, 16 * 128], BF16)
    onb = P.sbuf("onb", [128, 16 * TPC], BF16)
    xtl = P.sbuf("xtl", [128, 512], F32)
    xtl2 = P.sbuf("xtl2", [128, 512], F32)
    r2b = P.sbuf("r2b", [128, TPC], F32)
    ds2r = P.sbuf("ds2rr", [1, TPC], F32)
    racc = P.sbuf("racc", [128, 64], F32)
    rcol = P.sbuf("rcol", [128, 1], F32)
    rsc = P.sbuf("rsc", [128, 1], F32)
    soutc = P.sbuf("soutc", [128, 1], F32)
    soutr = P.sbuf("soutr", [1, 1], F32)
    bos = P.sbuf("bos", [128, 16], F32)
    t1 = P.sbuf("t1", [128, 512], F32)
    t2 = P.sbuf("t2", [128, 512], F32)
    t3 = P.sbuf("t3", [128, 512], F32)
    ones1 = P.sbuf("ones1", [1, 128], F32)
    onesc = P.sbuf("onesc", [128, 1], F32)
    pa = P.psum("pa", [128, 512], F32)
    pb = P.psum("pb", [128, 512], F32)
    pr = P.psum("pr", [1, 512], F32)
    dep.dma("y", SY, ds2r, dds2[:, :], writes=["ds2r"])
    dep.dma("y", SY, bos, dbo[:, :], writes=["bos"])
    dep.dma("g", GP, onb.rearrange("p (dt t) -> p dt t", dt=16),
            don1.rearrange("(dt p) t -> p dt t", p=128), writes=["onb"])
    dep.op("v", V, lambda: V.memset(ones1, 1.0), writes=["ones1"])
    dep.op("v", V, lambda: V.memset(onesc, 1.0), writes=["onesc"])
    dep.op("v", V, lambda: V.memset(racc, 0.0), writes=["racc"])
    dep.fence(V, "v")
    for g4 in range(4):
        for sub in range(8):
            dep.dma("g", GP, wsl.rearrange("p (dt o) -> p dt o", dt=8),
                    dwout[1024 * sub:1024 * (sub + 1),
                          512 * g4:512 * (g4 + 1)].rearrange("(dt p) o -> p dt o", p=128),
                    writes=["wsl"])
            dep.op("v", V, lambda g4=g4, sub=sub: V.reduce_sum(
                racc[:, g4 * 8 + sub:g4 * 8 + sub + 1], wsl, axis=AX,
                apply_absolute_value=True),
                reads=["wsl"], writes=["racc"])
    dep.op("v", V, lambda: V.reduce_sum(rcol, racc[:, 0:32], axis=AX),
           reads=["racc"], writes=["rcol"])
    dep.op("t", TE, lambda: TE.matmul(pr[:, 0:1], rcol, onesc, start=True, stop=True),
           reads=["rcol", "onesc"], writes=["pr"])
    dep.op("v", V, lambda: V.tensor_scalar(pr[0:1, 0:1], pr[0:1, 0:1], 1.0 / (8192.0 * D),
                                           None, ALU.mult),
           reads=["pr"], writes=["pr"], force_same=True)
    dep.op("v", V, lambda: V.tensor_scalar(pr[0:1, 0:1], pr[0:1, 0:1], 1e-5, None, ALU.max),
           reads=["pr"], writes=["pr"], force_same=True)
    dep.op("s", SC, lambda: SC.copy(soutr, pr[0:1, 0:1]), reads=["pr"], writes=["soutr"])
    dep.op("t", TE, lambda: TE.matmul(pb[:, 0:1], ones1, soutr, start=True, stop=True),
           reads=["soutr", "ones1"], writes=["pb"])
    dep.op("v", V, lambda: V.tensor_copy(soutc, pb[:, 0:1]), reads=["pb"], writes=["soutc"])
    dep.op("v", V, lambda: V.reciprocal(rsc, soutc), reads=["soutc"], writes=["rsc"])
    for a in range(TPC // 512):
        dep.op("t", TE, lambda a=a: TE.matmul(pb[:, 0:512], ones1,
                                              ds2r[:, 512 * a:512 * (a + 1)],
                                              start=True, stop=True),
               reads=["ds2r", "ones1"], writes=["pb"])
        dep.op("v", V, lambda a=a: V.tensor_scalar(r2b[:, 512 * a:512 * (a + 1)], pb[:, 0:512],
                                                   soutc, None, ALU.mult),
               reads=["pb", "soutc"], writes=["r2b"])
    for g4 in range(4):
        for sub in range(8):
            dep.dma("g", GP, wsl.rearrange("p (dt o) -> p dt o", dt=8),
                    dwout[1024 * sub:1024 * (sub + 1),
                          512 * g4:512 * (g4 + 1)].rearrange("(dt p) o -> p dt o", p=128),
                    writes=["wsl"])
            dep.op("v", V, lambda: V.tensor_scalar(qi, wsl, rsc, None, ALU.mult),
                   reads=["wsl", "rsc"], writes=["qi"])
            dep.op("v", V, lambda sub=sub: V.tensor_scalar(wq1[:, sub * 4096:(sub + 1) * 4096],
                                                           qi, -1.0, 1.0, ALU.max, ALU.min),
                   reads=["qi"], writes=["wq1"])
        for tc in range(TPC // 512):
            dep.dma("g", GP, qa2c.rearrange("p (dt t) -> p dt t", dt=64),
                    dqa2[:, 512 * tc:512 * (tc + 1)].rearrange("(dt p) t -> p dt t", p=128),
                    writes=["qa2c"])
            for osub in range(4):
                ot = g4 * 4 + osub
                dep.dma("g", GP, won.rearrange("p (dt o) -> p dt o", dt=16),
                        dwoT[:, 128 * ot:128 * (ot + 1)].rearrange("(dt p) o -> p dt o", p=128),
                        writes=["won"])
                def mm2(osub=osub):
                    last = None
                    for dt in range(64):
                        last = TE.matmul(pa[:, 0:512],
                                         wq1[:, dt * 512 + osub * 128: dt * 512 + osub * 128 + 128],
                                         qa2c[:, dt * 512: (dt + 1) * 512],
                                         start=(dt == 0), stop=(dt == 63))
                    return last
                dep.op("t", TE, mm2, reads=["wq1", "qa2c"], writes=["pa"])
                def mm1(ot=ot, tc=tc):
                    last = None
                    for dt in range(16):
                        last = TE.matmul(pb[:, 0:512], won[:, dt * 128: dt * 128 + 128],
                                         onb[:, dt * TPC + tc * 512: dt * TPC + tc * 512 + 512],
                                         start=(dt == 0), stop=(dt == 15))
                    return last
                dep.op("t", TE, mm1, reads=["won", "onb"], writes=["pb"])
                sl = slice(512 * tc, 512 * (tc + 1))
                dep.dma("y", SY, xtl, dxn1[128 * ot:128 * (ot + 1), sl], writes=["xtl"])
                dep.dma("y", SY, xtl2, dx1[128 * ot:128 * (ot + 1), sl], writes=["xtl2"])
                dep.op("v", V, lambda sl=sl: V.tensor_tensor(t1, pa[:, 0:512], r2b[:, sl],
                                                             ALU.mult),
                       reads=["pa", "r2b"], writes=["t1"])
                dep.op("v", V, lambda: V.tensor_tensor(t1, t1, xtl, ALU.add),
                       reads=["t1", "xtl"], writes=["t1"])
                dep.op("s", SC, lambda: SC.activation(t2, t1, AF.Silu), reads=["t1"],
                       writes=["t2"])
                dep.op("v", V, lambda ot=ot: V.tensor_scalar(t3, pb[:, 0:512],
                                                             bos[:, ot:ot + 1], None, ALU.add),
                       reads=["pb", "bos"], writes=["t3"])
                dep.op("v", V, lambda: V.tensor_tensor(t3, t3, t2, ALU.mult),
                       reads=["t3", "t2"], writes=["t3"])
                dep.op("v", V, lambda: V.tensor_tensor(t3, t3, xtl2, ALU.add),
                       reads=["t3", "xtl2"], writes=["t3"])
                dep.dma("y", SY, dout[128 * ot:128 * (ot + 1), sl], t3, reads=["t3"])
    return P.finish()


def kernel(**g):
    cores = list(range(8))
    f32 = np.float32
    asc = np.ascontiguousarray

    def A(name):
        return np.asarray(g[name], dtype=f32)

    x = A("L0_attn_hidden_states")
    xf = x.reshape(B * T, D)
    rows = [slice(1024 * c, 1024 * (c + 1)) for c in cores]

    nc1 = build_ln(False)
    res = run_bass_kernel_spmd(nc1, [{"x": asc(xf[rows[c]])} for c in cores], core_ids=cores)
    xnf = np.concatenate([res.results[c]["xn"] for c in cores], 0)

    Wq, Wk, Wv, Wg = A("L0_attn_Wq"), A("L0_attn_Wk"), A("L0_attn_Wv"), A("L0_attn_Wg")
    Wb, Wf1, Wf2 = A("L0_attn_Wb"), A("L0_attn_Wf1"), A("L0_attn_Wf2")
    cq, ck, cv = A("L0_attn_conv_q"), A("L0_attn_conv_k"), A("L0_attn_conv_v")
    alog, dtb, bgv = A("L0_attn_A_log"), A("L0_attn_dt_bias"), A("L0_attn_bg")
    nc2 = build_l0proj()
    maps = []
    for c in cores:
        b, hg = c // 2, c % 2
        hs = slice(1024 * hg, 1024 * (hg + 1))
        maps.append({
            "xnT": asc(xnf[2048 * b:2048 * (b + 1)].T),
            "wq": asc(Wq[:, hs]), "wk": asc(Wk[:, hs]), "wv": asc(Wv[:, hs]),
            "wg": asc(Wg[:, hs]), "wb": asc(Wb[:, 8 * hg:8 * (hg + 1)]),
            "wf1": Wf1, "wf2": asc(Wf2[:, hs]),
            "cq": asc(cq[hs].reshape(8, 128, 4).transpose(1, 0, 2).reshape(128, 32)),
            "ck": asc(ck[hs].reshape(8, 128, 4).transpose(1, 0, 2).reshape(128, 32)),
            "cv": asc(cv[hs].reshape(8, 128, 4).transpose(1, 0, 2).reshape(128, 32)),
            "dtb": asc(dtb[hs].reshape(8, 128).T),
            "alog": asc(alog[8 * hg:8 * (hg + 1)].reshape(1, 8)),
            "bgv": asc(bgv[hs].reshape(8, 128).T),
        })
    res2 = run_bass_kernel_spmd(nc2, maps, core_ids=cores)

    onw = A("L0_attn_o_norm_w")
    nc3 = build_delta()
    maps = []
    for c in cores:
        r = res2.results[c]
        beta8 = r["beta8"]
        rq8 = r["rq8"]
        gate8 = r["gate8"]
        maps.append({
            "q8": r["q8"], "k8": r["k8"], "v8": r["v8"], "ld8": r["ld8"],
            "beta64": asc(np.concatenate([beta8[h].reshape(32, 64).T for h in range(8)], 0)),
            "rqtok": asc(np.concatenate(
                [rq8[0, 2048 * h:2048 * (h + 1)].reshape(16, 128).T for h in range(8)], 0)),
            "gatetok": asc(np.concatenate(
                [gate8[128 * h:128 * (h + 1)].T.reshape(16, 128, 128)
                 .transpose(1, 0, 2).reshape(128, 2048) for h in range(8)], 0)),
            "onw": asc(onw.reshape(1, 128)),
        })
    res3 = run_bass_kernel_spmd(nc3, maps, core_ids=cores)
    on = np.zeros((B, T, D), dtype=f32)
    for c in cores:
        b, hg = c // 2, c % 2
        on8 = res3.results[c]["on8"]
        for h in range(8):
            slab = on8[128 * h:128 * (h + 1)]
            on[b, :, 1024 * hg + 128 * h:1024 * hg + 128 * (h + 1)] = (
                slab.reshape(128, 16, 128).transpose(1, 0, 2).reshape(2048, 128))
    onf = on.reshape(B * T, D)

    nc4 = build_ffn0()
    w1T = asc(A("L0_ffn_w1").transpose(0, 2, 1))
    w2T = asc(A("L0_ffn_w2").transpose(0, 2, 1))
    gwT = asc(A("L0_ffn_gate_w").T)
    gbr = asc(A("L0_ffn_gate_bias").reshape(1, 32))
    fc1T = asc(A("L0_ffn_fc1_w").T)
    fc2T = asc(A("L0_ffn_fc2_w").T)
    suT = asc(A("L0_ffn_shared_up_w").T)
    sdT = asc(A("L0_ffn_shared_down_w").T)
    wo0 = A("L0_attn_Wo")
    maps = []
    for c in cores:
        maps.append({
            "xnT": asc(xnf[rows[c]].T), "xT": asc(xf[rows[c]].T),
            "onT": asc(onf[rows[c]].T), "wo0": wo0,
            "gwT": gwT, "gb": gbr, "fc1T": fc1T, "w1T": w1T, "w2T": w2T,
            "fc2T": fc2T, "suT": suT, "sdT": sdT,
        })
    res4 = run_bass_kernel_spmd(nc4, maps, core_ids=cores)
    x1f = np.concatenate([res4.results[c]["x1T"].T for c in cores], 0)

    nc5 = build_ln(True)
    gam = asc(A("L1_ffn_norm_w").reshape(1, D))
    maps = [{"x": asc(x1f[rows[c]]), "gam": gam} for c in cores]
    res5 = run_bass_kernel_spmd(nc5, maps, core_ids=cores)
    xn1f = np.concatenate([res5.results[c]["xn"] for c in cores], 0)
    qa1f = np.concatenate([res5.results[c]["qa"] for c in cores], 0)

    nc6 = build_mha()
    Wqkv, bqkv = A("L1_attn_Wqkv"), A("L1_attn_bqkv")
    maps = []
    for c in cores:
        b, hg = c // 2, c % 2
        hs = slice(1024 * hg, 1024 * (hg + 1))
        maps.append({
            "xn1T": asc(xn1f[2048 * b:2048 * (b + 1)].T),
            "wqT": asc(Wqkv[hs].T),
            "wkT": asc(Wqkv[2048 + 1024 * hg:2048 + 1024 * (hg + 1)].T),
            "wvT": asc(Wqkv[4096 + 1024 * hg:4096 + 1024 * (hg + 1)].T),
            "bq": asc(bqkv[hs].reshape(8, 128).T),
            "bk": asc(bqkv[2048 + 1024 * hg:2048 + 1024 * (hg + 1)].reshape(8, 128).T),
            "bv": asc(bqkv[4096 + 1024 * hg:4096 + 1024 * (hg + 1)].reshape(8, 128).T),
        })
    res6 = run_bass_kernel_spmd(nc6, maps, core_ids=cores)
    on1 = np.zeros((B, T, D), dtype=f32)
    for c in cores:
        b, hg = c // 2, c % 2
        o1 = res6.results[c]["on1T"]
        for h in range(8):
            on1[b, :, 1024 * hg + 128 * h:1024 * hg + 128 * (h + 1)] = o1[128 * h:128 * (h + 1)].T
    on1f = on1.reshape(B * T, D)

    nc7 = build_bl1()
    wgvT = asc(A("L1_ffn_w_gv").T)
    maps = []
    for c in cores:
        ds0 = res5.results[c]["ds0"]
        maps.append({
            "qaT": asc(qa1f[rows[c]].T), "wgvT": wgvT,
            "ds0r": asc(ds0.T.reshape(1, TPC)),
        })
    res7 = run_bass_kernel_spmd(nc7, maps, core_ids=cores)

    nc8 = build_quant2()
    maps = []
    for c in cores:
        maps.append({
            "htil": asc(res7.results[c]["htilT"].T),
            "ds0": res5.results[c]["ds0"],
            "sgv": res7.results[c]["sgv"],
        })
    res8 = run_bass_kernel_spmd(nc8, maps, core_ids=cores)

    nc9 = build_bl2()
    woutT = asc(A("L1_ffn_w_out").T)
    woT = asc(A("L1_attn_Wo").T)
    bo = asc(A("L1_attn_bo").reshape(16, 128).T)
    maps = []
    for c in cores:
        maps.append({
            "qa2T": asc(res8.results[c]["qa2"].T), "woutT": woutT,
            "ds2r": asc(res8.results[c]["ds2"].T.reshape(1, TPC)),
            "xn1T": asc(xn1f[rows[c]].T), "x1T": asc(x1f[rows[c]].T),
            "on1T": asc(on1f[rows[c]].T), "woT": woT, "bo": bo,
        })
    res9 = run_bass_kernel_spmd(nc9, maps, core_ids=cores)
    out = np.concatenate([res9.results[c]["outT"].T for c in cores], 0)
    return out.reshape(B, T, D).astype(np.float32)
